# version 26
# baseline (speedup 1.0000x reference)
import numpy as np
import contextlib
import concourse.bass as bass
import concourse.mybir as mybir
from concourse.bass_utils import run_bass_kernel_spmd

F32 = mybir.dt.float32
BF16 = mybir.dt.bfloat16
AF = mybir.ActivationFunctionType
ALU = mybir.AluOpType
AX = mybir.AxisListType

NCORES = 8
T = 4096
G = 8
TC = 512
NCH = T // TC

ENGS = ("tensor", "vector", "scalar", "gpsimd", "sync")
NLANES = 6
ATTN_LVL = 6


class Res:
    __slots__ = ("name", "lw", "readers")
    def __init__(self, name="r"):
        self.name = name
        self.lw = None
        self.readers = []

class _Op:
    __slots__ = ("idx", "eng", "fn", "dma", "lane", "wait_val", "throttle",
                 "drain_need", "waits", "marked", "rank", "eseq")
    def __init__(self, idx, eng, fn, dma):
        self.idx = idx
        self.eng = eng
        self.fn = fn
        self.dma = dma
        self.lane = 0
        self.wait_val = 0
        self.throttle = 0
        self.drain_need = -1
        self.waits = []
        self.marked = False
        self.rank = 0
        self.eseq = 0

class Prog:
    def __init__(self):
        self.ops = []
        self.eops = {e: [] for e in ENGS}
        self.dmacount = {e: 0 for e in ENGS}

    def add(self, eng, fn, reads=(), writes=(), dma=False):
        op = _Op(len(self.ops), eng, fn, dma)
        op.eseq = len(self.eops[eng])
        if dma:
            n = self.dmacount[eng]
            self.dmacount[eng] = n + 1
            op.lane = n % NLANES
            uses = n // NLANES
            op.throttle = 16 * uses
            op.wait_val = 16 * (uses + 1)
        deps = {}
        for r in reads:
            if r.lw is not None:
                deps[r.lw.idx] = r.lw
        for r in writes:
            if r.lw is not None:
                deps[r.lw.idx] = r.lw
            for rd in r.readers:
                deps[rd.idx] = rd
        for pidx in sorted(deps):
            p = deps[pidx]
            if p is op:
                continue
            if p.dma:
                op.waits.append(("dma", p.eng, p.lane, p.wait_val))
            elif p.eng == eng:
                if eng == "tensor" and not dma:
                    pass
                else:
                    if p.eseq > op.drain_need:
                        op.drain_need = p.eseq
            else:
                p.marked = True
                op.waits.append(("eng", p.eng, p))
        for r in writes:
            r.lw = op
            r.readers = []
        for r in reads:
            r.readers.append(op)
        self.ops.append(op)
        self.eops[eng].append(op)
        return op

    @staticmethod
    def _lane_totals(n):
        return [16 * ((n - l + NLANES - 1) // NLANES) if n > l else 0 for l in range(NLANES)]

    def fence_close(self, key):
        bars = [Res(f"bar_{key}_{e}") for e in ENGS]
        dmasnap = dict(self.dmacount)
        for e in ENGS:
            snap = dmasnap[e]
            def fnop(eng, e=e, snap=snap, prog=self):
                for l, v in enumerate(prog._lane_totals(snap)):
                    if v:
                        eng.wait_ge(prog._dma_sems[e][l], v)
                eng.drain()
                return eng.nop()
            self.add(e, fnop, reads=(), writes=(bars[ENGS.index(e)],))
        self._fences = getattr(self, "_fences", {})
        self._fences[key] = tuple(bars)

    def fence_open(self, key):
        allb = self._fences[key]
        for e in ENGS:
            self.add(e, lambda eng: eng.nop(), reads=allb, writes=())

    def barrier(self):
        self.fence_close(f"bar{len(self.ops)}")
        self.fence_open(f"bar{len(self.ops) - 10}" if False else list(self._fences)[-1])

    def emit(self, nc, st):
        self._eng_sems = {e: st.enter_context(nc.semaphore(f"es_{e}")) for e in ENGS}
        self._dma_sems = {e: [st.enter_context(nc.semaphore(f"ds_{e}{l}"))
                              for l in range(NLANES)] for e in ENGS}
        for e in ENGS:
            r = 0
            for op in self.eops[e]:
                if op.marked:
                    r += 1
                    op.rank = r
        prog = self
        with nc.Block() as block:
            def make_body(e):
                def body(eng):
                    drained = -1
                    waited = {}
                    last_seq = -1
                    for op in prog.eops[e]:
                        if op.drain_need > drained:
                            eng.drain()
                            drained = last_seq
                        for w in op.waits:
                            if w[0] == "dma":
                                key = ("d", w[1], w[2]); v = w[3]
                                sem = prog._dma_sems[w[1]][w[2]]
                            else:
                                key = ("e", w[1]); v = w[2].rank
                                sem = prog._eng_sems[w[1]]
                            if waited.get(key, 0) < v:
                                eng.wait_ge(sem, v)
                                waited[key] = v
                        if op.dma and op.throttle:
                            key = ("d", e, op.lane)
                            if waited.get(key, 0) < op.throttle:
                                eng.wait_ge(prog._dma_sems[e][op.lane], op.throttle)
                                waited[key] = op.throttle
                        inst = op.fn(eng)
                        if op.dma:
                            inst.then_inc(prog._dma_sems[e][op.lane], 16)
                        elif op.marked:
                            inst.then_inc(prog._eng_sems[e], 1)
                        last_seq = op.eseq
                    if e == "gpsimd":
                        for q in ENGS:
                            for l, v in enumerate(prog._lane_totals(prog.dmacount[q])):
                                if v:
                                    eng.wait_ge(prog._dma_sems[q][l], v)
                return body
            block.tensor(make_body("tensor"))
            block.vector(make_body("vector"))
            block.scalar(make_body("scalar"))
            block.gpsimd(make_body("gpsimd"))
            block.sync(make_body("sync"))



class KB:
    def __init__(self, nc, P, st):
        self.nc = nc
        self.P = P
        self.st = st
        self.cur_st = st
        self.d = {}
        self.uid = 0

    def tile(self, name, shape, dtype, nres=1):
        h = self.cur_st.enter_context(self.nc.sbuf_tensor(
            name, list(shape), dtype, side=getattr(self, "cur_side", "left")))
        rs = [Res(f"{name}.{i}") for i in range(nres)]
        return h, rs

    @contextlib.contextmanager
    def scope(self, side="left"):
        prev = self.cur_st
        prev_side = getattr(self, "cur_side", "left")
        with contextlib.ExitStack() as st2:
            self.cur_st = st2
            self.cur_side = side
            try:
                yield
            finally:
                self.cur_st = prev
                self.cur_side = prev_side

    def dram_in(self, name, shape, dtype=F32):
        t = self.nc.dram_tensor(name, list(shape), dtype, kind="ExternalInput")
        self.d[name] = t.ap()
        return self.d[name]

    def dram_out(self, name, shape, dtype=F32):
        t = self.nc.dram_tensor(name, list(shape), dtype, kind="ExternalOutput")
        self.d[name] = t.ap()
        return self.d[name]

    def alloc_global(self):
        nc, P, st = self.nc, self.P, self.st
        self.xs, _ = self.tile("xs", [128, G, T], BF16)
        self.xn, _ = self.tile("xn", [128, G, T], BF16)
        self.xs_r = [[Res(f"xs.{g}.{c}") for c in range(NCH)] for g in range(G)]
        self.xn_r = [[Res(f"xn.{g}.{c}") for c in range(NCH)] for g in range(G)]
        self.pbH = [st.enter_context(nc.psum_tensor(f"pbH{j}", [128, 1024], F32)) for j in range(4)]
        self.pb = [self.pbH[i // 2][:, (i % 2) * 512:(i % 2) * 512 + 512] for i in range(8)]
        self.pb_r = [Res(f"pb{i}") for i in range(8)]
        self.ident, self.ident_r = self.tile("ident", [128, 128], F32)
        self.ones_col_bf, self.ocb_r = self.tile("ones_col_bf", [128, 4], BF16)
        self.ones_row_bf, self.orb_r = self.tile("ones_row_bf", [1, 128], BF16)
        self.ones_row_f32, self.orf_r = self.tile("ones_row_f32", [1, 128], F32)
        self.ones_col_f32, self.ocf_r = self.tile("ones_col_f32", [128, 1], F32)
        self.sel4, self.sel4_r = self.tile("sel4", [128, 16], BF16)
        self.self4, self.self4_r = self.tile("self4", [4, 512], BF16)
        self.fstg, self.fstg_r = self.tile("fstg", [128, 2, 512], F32, 2)
        self.srow, self.srow_r = self.tile("srow", [1, 4, 512], F32, 4)
        self.col1, self.col1_r = self.tile("col1", [128, 16], F32, 16)
        self.sq16, self.sq16_r = self.tile("sq16", [128, 4, 16], F32, 4)
        self.cbias, self.cbias_r = self.tile("cbias", [128, 3], F32)

        P.add("gpsimd", lambda e: e.iota(self.fstg[:, 0, 0:128], [[1, 128]], channel_multiplier=0,
                                         allow_small_or_imprecise_dtypes=True),
              writes=(self.fstg_r[0],))
        P.add("gpsimd", lambda e: e.iota(self.fstg[:, 1, 0:128], [[0, 128]], channel_multiplier=1,
                                         allow_small_or_imprecise_dtypes=True),
              writes=(self.fstg_r[1],))
        P.add("vector", lambda e: e.tensor_tensor(out=self.ident[:], in0=self.fstg[:, 0, 0:128],
                                                  in1=self.fstg[:, 1, 0:128], op=ALU.is_equal),
              reads=(self.fstg_r[0], self.fstg_r[1]), writes=(self.ident_r[0],))
        P.add("vector", lambda e: e.memset(self.ones_col_bf[:], 1.0), writes=(self.ocb_r[0],))
        P.add("vector", lambda e: e.memset(self.ones_row_bf[:], 1.0), writes=(self.orb_r[0],))
        P.add("vector", lambda e: e.memset(self.ones_row_f32[:], 1.0), writes=(self.orf_r[0],))
        P.add("vector", lambda e: e.memset(self.ones_col_f32[:], 1.0), writes=(self.ocf_r[0],))
        def cb_fill(e):
            e.memset(self.cbias[:, 0:1], 1e-5)
            e.memset(self.cbias[:, 1:2], 1e-6)
            return e.memset(self.cbias[:, 2:3], 27.7258872223978)
        P.add("vector", cb_fill, writes=(self.cbias_r[0],))
        def sel4_build(e):
            i = e.memset(self.sel4[:], 0.0)
            for h in range(4):
                for g in (2 * h, 2 * h + 1):
                    i = e.memset(self.sel4[128 * 0:128, 4 * h + g // 2:4 * h + g // 2 + 1], 0.0)
            return i
        def sel4_fill(e):
            e.memset(self.sel4[:], 0.0)
            e.drain()
            for h in range(4):
                i = e.memset(self.sel4[:, 4 * h + h:4 * h + h + 1], 1.0)
            return i
        P.add("vector", sel4_fill, writes=(self.sel4_r[0],))
        P.add("gpsimd", lambda e: e.iota(self.fstg[0:4, 0, 0:512], [[1, 4], [0, 128]],
                                         channel_multiplier=0, allow_small_or_imprecise_dtypes=True),
              writes=(self.fstg_r[0],))
        P.add("gpsimd", lambda e: e.iota(self.fstg[0:4, 1, 0:512], [[0, 4], [0, 128]],
                                         channel_multiplier=1, allow_small_or_imprecise_dtypes=True),
              writes=(self.fstg_r[1],))
        P.add("vector", lambda e: e.tensor_tensor(out=self.self4[:], in0=self.fstg[0:4, 0, 0:512],
                                                  in1=self.fstg[0:4, 1, 0:512], op=ALU.is_equal),
              reads=(self.fstg_r[0], self.fstg_r[1]), writes=(self.self4_r[0],))

    def load_w(self, dram2d, dest, dest_res, kgs, M, dest_kg0=0, scale=None, tag=""):
        P = self.P
        if scale is None:
            src = dram2d[0:kgs * 128, 0:M].rearrange("(kg p) m -> p kg m", p=128)
            P.add("gpsimd", lambda e, src=src: e.dma_start(
                out=dest[:, dest_kg0:dest_kg0 + kgs, 0:M], in_=src),
                writes=(dest_res,), dma=True)
            return
        nchunk = (M + 255) // 256
        si = 0
        for kg in range(kgs):
            for cc in range(nchunk):
                c0 = cc * 256
                c1 = min(M, c0 + 256)
                slot = si % 2
                si += 1
                src = dram2d[kg * 128:(kg + 1) * 128, c0:c1]
                P.add("sync", lambda e, src=src, slot=slot, w=c1 - c0:
                      e.dma_start(out=self.stg[:, slot, 0:w], in_=src),
                      writes=(self.stg_r[slot],), dma=True)
                P.add("gpsimd", lambda e, kg=kg, c0=c0, c1=c1, slot=slot:
                      e.tensor_scalar(out=dest[:, dest_kg0 + kg, c0:c1],
                                      in0=self.stg[:, slot, 0:c1 - c0],
                                      scalar1=scale, scalar2=None, op0=ALU.mult),
                      reads=(self.stg_r[slot],), writes=(dest_res,))

    def load_params(self, need):
        nc, P = self.nc, self.P
        if "l0" in need:
            self.qnw, self.qnw_r = self.tile("qnw", [128, 8], F32)
            ap = self.dram_in("qnw_pp", [128, 8])
            P.add("sync", lambda e: e.dma_start(out=self.qnw[:], in_=ap[:]),
                  writes=(self.qnw_r[0],), dma=True)
        if "moe" in need:
            self.cent_b, self.cent_r = self.tile("cent_b", [128, G, 16], BF16)
            capf = self.dram_in("centT", [1024, 16])
            P.add("gpsimd", lambda e: e.dma_start(
                out=self.cent_b[:], in_=capf.rearrange("(kg p) m -> p kg m", p=128)),
                writes=(self.cent_r[0],), dma=True)
            self.ebias_bc, self.ebias_r = self.tile("ebias_bc", [128, 16], F32)
            ebap = self.dram_in("ebias", [1, 16])
            self.ebrow, self.ebrow_r = self.tile("ebrow", [1, 16], F32)
            P.add("sync", lambda e: e.dma_start(out=self.ebrow[:], in_=ebap[:]),
                  writes=(self.ebrow_r[0],), dma=True)
            P.add("tensor", lambda e: e.matmul(out=self.pb[0][0:128, 0:16],
                                               lhsT=self.ones_row_f32[:],
                                               rhs=self.ebrow[:], start=True, stop=True),
                  reads=(self.orf_r[0], self.ebrow_r[0]), writes=(self.pb_r[0],))
            P.add("vector", lambda e: e.tensor_copy(out=self.ebias_bc[:], in_=self.pb[0][0:128, 0:16]),
                  reads=(self.pb_r[0],), writes=(self.ebias_r[0],))
            self.br_b, self.br_r = self.tile("br_b", [16, 1024], BF16)
            brap = self.dram_in("brr", [16, 1024])
            P.add("gpsimd", lambda e: e.dma_start(out=self.br_b[:], in_=brap[:]),
                  writes=(self.br_r[0],), dma=True)
            self.bsh, self.bsh_r = self.tile("bsh", [128, 8], F32)
            self.bstmp, self.bstmp_r = self.tile("bstmp", [128, 2, 8], F32)
            bsap = self.dram_in("bs_pp", [2, 128, 8])
            P.add("sync", lambda e: e.dma_start(out=self.bstmp[:], in_=bsap[:].rearrange("e p g -> p e g")),
                  writes=(self.bstmp_r[0],), dma=True)
            def bs_build(e):
                e.tensor_tensor(out=self.bsh[:], in0=self.bstmp[:, 0, :], in1=self.bstmp[:, 1, :], op=ALU.add)
                e.drain()
                return e.tensor_scalar(out=self.bsh[:], in0=self.bsh[:], scalar1=0.5, scalar2=None, op0=ALU.mult)
            P.add("vector", bs_build, reads=(self.bstmp_r[0],), writes=(self.bsh_r[0],))
        if "l1" in need:
            self.gp, self.gp_r = self.tile("gp_sb", [128, 8], F32)
            gap = self.dram_in("gamma_pp", [128, 8])
            self.gstg, self.gstg_r = self.tile("gstg", [128, 8], F32)
            P.add("sync", lambda e: e.dma_start(out=self.gstg[:], in_=gap[:]),
                  writes=(self.gstg_r[0],), dma=True)
            P.add("vector", lambda e: e.tensor_scalar(out=self.gp[:], in0=self.gstg[:],
                                                      scalar1=1.0, scalar2=32.0,
                                                      op0=ALU.add, op1=ALU.mult),
                  reads=(self.gstg_r[0],), writes=(self.gp_r[0],))
        if "ffn1" in need:
            self.bias_pp, self.bias_r = self.tile("bias_sb", [128, 8], F32)
            bap = self.dram_in("bias_pp", [128, 8])
            P.add("sync", lambda e: e.dma_start(out=self.bias_pp[:], in_=bap[:]),
                  writes=(self.bias_r[0],), dma=True)
            self.alpha_t, self.alpha_r = self.tile("alpha_t", [1, 1], F32)
            aap = self.dram_in("alpha", [1, 1])
            P.add("sync", lambda e: e.dma_start(out=self.alpha_t[:], in_=aap[:]),
                  writes=(self.alpha_r[0],), dma=True)
            self.alph_bc, self.alphbc_r = self.tile("alph_bc", [128, 1], F32)
            P.add("tensor", lambda e: e.matmul(out=self.pb[1][0:128, 0:1],
                                               lhsT=self.ones_row_f32[:], rhs=self.alpha_t[:],
                                               start=True, stop=True),
                  reads=(self.orf_r[0], self.alpha_r[0]), writes=(self.pb_r[1],))
            P.add("vector", lambda e: e.tensor_copy(out=self.alph_bc[:], in_=self.pb[1][0:128, 0:1]),
                  reads=(self.pb_r[1],), writes=(self.alphbc_r[0],))

    def load_x_fp32(self):
        ap = self.dram_in("xT", [1024, T])
        P = self.P
        for g in range(G):
            for ch in range(0, NCH, 4):
                P.add("gpsimd", lambda e, g=g, ch=ch: e.dma_start(
                    out=self.xs[:, g, ch * TC:(ch + 4) * TC],
                    in_=ap[g * 128:(g + 1) * 128, ch * TC:(ch + 4) * TC]),
                    writes=tuple(self.xs_r[g][ch:ch + 4]), dma=True)

    def load_bf(self, name, dest, dres):
        ap = self.dram_in(name, [1024, T], BF16)
        P = self.P
        for g in range(G):
            for ch in range(0, NCH, 4):
                P.add("sync" if g % 2 == 0 else "gpsimd",
                      lambda e, g=g, ch=ch: e.dma_start(
                          out=dest[:, g, ch * TC:(ch + 4) * TC],
                          in_=ap[g * 128:(g + 1) * 128, ch * TC:(ch + 4) * TC]),
                      writes=tuple(dres[g][ch:ch + 4]), dma=True)

    def store_bf(self, name, src, sres):
        ap = self.dram_out(name, [1024, T], BF16)
        P = self.P
        for g in range(G):
            for ch in range(0, NCH, 4):
                P.add("sync" if g % 2 == 0 else "gpsimd",
                      lambda e, g=g, ch=ch: e.dma_start(
                          out=ap[g * 128:(g + 1) * 128, ch * TC:(ch + 4) * TC],
                          in_=src[:, g, ch * TC:(ch + 4) * TC]),
                      reads=tuple(sres[g][ch:ch + 4]), dma=True)

    def store_out_fp32(self):
        ap = self.dram_out("out", [1024, T])
        P = self.P
        for ch in range(0, NCH, 4):
            for g in range(G):
                P.add("gpsimd", lambda e, g=g, ch=ch: e.dma_start(
                    out=ap[g * 128:(g + 1) * 128, ch * TC:(ch + 4) * TC],
                    in_=self.xs[:, g, ch * TC:(ch + 4) * TC]),
                    reads=tuple(self.xs_r[g][ch:ch + 4]), dma=True)

    def phase_ln(self):
        P = self.P
        sq, sq_r = self.tile(f"lnsq{self.uid}", [128, 2, 512], BF16, 2)
        self.uid += 1
        for ch in range(NCH):
            par = ch % 2
            bA, bB, bR, bM = 4 * par, 4 * par + 1, 4 * par + 2, 4 * par + 3
            c0 = ch * TC
            cs = slice(c0, c0 + TC)
            for g in range(G):
                P.add("scalar", lambda e, g=g, cs=cs, par=par: e.square(
                    out=sq[:, par, :], in_=self.xs[:, g, cs]),
                    reads=(self.xs_r[g][ch],), writes=(sq_r[par],))
                P.add("tensor", lambda e, g=g, cs=cs, bA=bA: e.matmul(
                    out=self.pb[bA][0:1, :], lhsT=self.ones_col_bf[:, 0:1],
                    rhs=self.xs[:, g, cs], start=(g == 0), stop=(g == G - 1)),
                    reads=(self.ocb_r[0], self.xs_r[g][ch]), writes=(self.pb_r[bA],))
                P.add("tensor", lambda e, g=g, par=par, bB=bB: e.matmul(
                    out=self.pb[bB][0:1, :], lhsT=self.ones_col_bf[:, 0:1],
                    rhs=sq[:, par, :], start=(g == 0), stop=(g == G - 1)),
                    reads=(self.ocb_r[0], sq_r[par]), writes=(self.pb_r[bB],))
            sm = 0
            m_s, v_s, a_s, b_s = sm, sm + 1, sm + 2, sm + 3
            P.add("vector", lambda e, bA=bA, m_s=m_s: e.tensor_scalar(
                out=self.srow[:, m_s, :], in0=self.pb[bA][0:1, :], scalar1=1.0 / 1024, scalar2=None, op0=ALU.mult),
                reads=(self.pb_r[bA],), writes=(self.srow_r[m_s],))
            P.add("vector", lambda e, bB=bB, v_s=v_s: e.tensor_scalar(
                out=self.srow[:, v_s, :], in0=self.pb[bB][0:1, :], scalar1=1.0 / 1024, scalar2=None, op0=ALU.mult),
                reads=(self.pb_r[bB],), writes=(self.srow_r[v_s],))
            P.add("vector", lambda e, m_s=m_s, b_s=b_s: e.tensor_tensor(
                out=self.srow[:, b_s, :], in0=self.srow[:, m_s, :], in1=self.srow[:, m_s, :], op=ALU.mult),
                reads=(self.srow_r[m_s],), writes=(self.srow_r[b_s],))
            P.add("vector", lambda e, v_s=v_s, b_s=b_s: e.tensor_tensor(
                out=self.srow[:, v_s, :], in0=self.srow[:, v_s, :], in1=self.srow[:, b_s, :], op=ALU.subtract),
                reads=(self.srow_r[v_s], self.srow_r[b_s]), writes=(self.srow_r[v_s],))
            P.add("scalar", lambda e, v_s=v_s, a_s=a_s: e.activation(
                out=self.srow[:, a_s, :], in_=self.srow[:, v_s, :], func=AF.Sqrt, bias=self.cbias[0:1, 0:1]),
                reads=(self.srow_r[v_s], self.cbias_r[0]), writes=(self.srow_r[a_s],))
            P.add("vector", lambda e, a_s=a_s, b_s=b_s: e.reciprocal(
                out=self.srow[:, b_s, :], in_=self.srow[:, a_s, :]),
                reads=(self.srow_r[a_s],), writes=(self.srow_r[b_s],))
            P.add("vector", lambda e, m_s=m_s, b_s=b_s, a_s=a_s: e.tensor_tensor(
                out=self.srow[:, a_s, :], in0=self.srow[:, m_s, :], in1=self.srow[:, b_s, :], op=ALU.mult),
                reads=(self.srow_r[m_s], self.srow_r[b_s]), writes=(self.srow_r[a_s],))
            r_s, mr_s = b_s, a_s
            P.add("tensor", lambda e, r_s=r_s, bR=bR: e.matmul(
                out=self.pb[bR][:], lhsT=self.ones_row_f32[:], rhs=self.srow[:, r_s, :],
                start=True, stop=True),
                reads=(self.orf_r[0], self.srow_r[r_s]), writes=(self.pb_r[bR],))
            P.add("tensor", lambda e, mr_s=mr_s, bM=bM: e.matmul(
                out=self.pb[bM][:], lhsT=self.ones_row_f32[:], rhs=self.srow[:, mr_s, :],
                start=True, stop=True),
                reads=(self.orf_r[0], self.srow_r[mr_s]), writes=(self.pb_r[bM],))
            for g in range(G):
                fs = par
                P.add("vector", lambda e, g=g, cs=cs, bR=bR, fs=fs: e.tensor_tensor(
                    out=self.fstg[:, fs, :], in0=self.xs[:, g, cs], in1=self.pb[bR][:], op=ALU.mult),
                    reads=(self.xs_r[g][ch], self.pb_r[bR]), writes=(self.fstg_r[fs],))
                P.add("vector", lambda e, g=g, cs=cs, bM=bM, fs=fs: e.tensor_tensor(
                    out=self.xn[:, g, cs], in0=self.fstg[:, fs, :], in1=self.pb[bM][:], op=ALU.subtract),
                    reads=(self.fstg_r[fs], self.pb_r[bM]), writes=(self.xn_r[g][ch],))

    def phase_l0attn(self):
        nc, P = self.nc, self.P
        wq_b, wq_r = self.tile("wq_b", [128, G, 1024], BF16)
        self.load_w(self.dram_in("wqT", [1024, 1024]), wq_b, wq_r[0], G, 1024)
        w0ap = self.dram_in("w0T", [4, 256, 512])
        w2ap = self.dram_in("w2T", [4, 256, 512])
        w1ap = self.dram_in("w1T", [4, 512, 256])
        w0_b, w0_r = self.tile("w0_b", [128, 4, 2, 512], BF16)
        w2_b, w2_r = self.tile("w2_b", [128, 4, 2, 512], BF16)
        w1_b, w1_r = self.tile("w1_b", [128, 4, 4, 256], BF16)
        for h in range(4):
            self.load_w(w0ap[h], w0_b[:, h, :, :], w0_r[0], 2, 512)
            self.load_w(w2ap[h], w2_b[:, h, :, :], w2_r[0], 2, 512)
            self.load_w(w1ap[h], w1_b[:, h, :, :], w1_r[0], 4, 256)
        qw, qw_r = self.tile("qw", [128, G, 512], BF16, 1)
        sqq, sqq_r = self.tile("sqq", [128, 2, 512], BF16, 2)
        self.quad, self.quad_r = self.tile("quad", [4, 2, 512], F32, 2)
        self.quadb, self.quadb_r = self.tile("quadb", [4, 1, 512], BF16, 1)
        selw, selw_r = self.tile("selw", [128, G, 4], BF16, 1)
        P.add("vector", lambda e: e.memset(selw[:], 0.0), writes=(selw_r[0],))
        for g in range(G):
            P.add("scalar", lambda e, g=g: e.activation(
                out=selw[:, g, g // 2:g // 2 + 1], in_=self.qnw[:, g:g + 1], func=AF.Square),
                reads=(self.qnw_r[0],), writes=(selw_r[0],))
        sg, sg_r = self.tile("sg", [128, 2, 512], BF16, 2)
        zst, zst_r = self.tile("zst", [128, 2, 512], BF16, 2)

        for ch in range(NCH):
            c0 = ch * TC
            cs = slice(c0, c0 + TC)
            pend_s = None

            def mostats(mo, sl):
                P.add("tensor", lambda e, mo=mo, sl=sl: e.matmul(
                    out=self.pb[2][0:4, :], lhsT=self.ones_col_bf[:],
                    rhs=sqq[:, sl, :], start=(mo == 0), stop=(mo == G - 1)),
                    reads=(self.ocb_r[0], sqq_r[sl]), writes=(self.pb_r[2],))
                P.add("tensor", lambda e, mo=mo, sl=sl: e.matmul(
                    out=self.pb[3][0:4, :], lhsT=selw[:, mo, :],
                    rhs=sqq[:, sl, :], start=(mo == 0), stop=(mo == G - 1)),
                    reads=(selw_r[0], sqq_r[sl]), writes=(self.pb_r[3],))

            for mo in range(G):
                bq = mo % 2
                P.add("tensor", lambda e, mo=mo, cs=cs, bq=bq: [
                    e.matmul(out=self.pb[bq][:], lhsT=wq_b[:, kg, mo * 128:(mo + 1) * 128],
                             rhs=self.xn[:, kg, cs], start=(kg == 0), stop=(kg == G - 1))
                    for kg in range(G)][-1],
                    reads=(wq_r[0],) + tuple(self.xn_r[kg][ch] for kg in range(G)), writes=(self.pb_r[bq],))
                sl = mo % 2
                P.add("scalar", lambda e, mo=mo, bq=bq: e.activation(
                    out=qw[:, mo, :], in_=self.pb[bq][:], func=AF.Copy, bias=0.0, scale=self.qnw[:, mo:mo + 1]),
                    reads=(self.pb_r[bq], self.qnw_r[0]), writes=(qw_r[0],))
                P.add("scalar", lambda e, mo=mo, bq=bq, sl=sl: e.square(
                    out=sqq[:, sl, :], in_=self.pb[bq][:]),
                    reads=(self.pb_r[bq],), writes=(sqq_r[sl],))
                if pend_s is not None:
                    mostats(*pend_s)
                pend_s = (mo, sl)
            mostats(*pend_s)
            P.add("scalar", lambda e: e.activation(
                out=self.quad[:, 0, :], in_=self.pb[2][0:4, :], func=AF.Sqrt, bias=self.cbias[0:4, 1:2], scale=1.0 / 1024),
                reads=(self.pb_r[2], self.cbias_r[0]), writes=(self.quad_r[0],))
            P.add("vector", lambda e: e.reciprocal(out=self.quad[:, 1, :], in_=self.quad[:, 0, :]),
                  reads=(self.quad_r[0],), writes=(self.quad_r[1],))
            P.add("scalar", lambda e: e.activation(
                out=self.quad[:, 0, :], in_=self.pb[3][0:4, :], func=AF.Sqrt),
                reads=(self.pb_r[3],), writes=(self.quad_r[0],))
            def fchain(e):
                e.tensor_tensor(out=self.quad[:, 0, :], in0=self.quad[:, 0, :], in1=self.quad[:, 1, :], op=ALU.mult)
                e.drain()
                e.tensor_scalar(out=self.quad[:, 0, :], in0=self.quad[:, 0, :], scalar1=1e-5, scalar2=None, op0=ALU.add)
                e.drain()
                e.reciprocal(out=self.quad[:, 0, :], in_=self.quad[:, 0, :])
                e.drain()
                return e.tensor_tensor(out=self.quadb[:, 0, :], in0=self.quad[:, 1, :], in1=self.quad[:, 0, :], op=ALU.mult)
            P.add("vector", fchain, reads=(self.quad_r[0], self.quad_r[1]),
                  writes=(self.quad_r[0], self.quadb_r[0]))
            for g in range(G):
                P.add("tensor", lambda e, g=g: e.matmul(
                    out=self.pb[4][:], lhsT=self.self4[:, 128 * (g // 2):128 * (g // 2) + 128],
                    rhs=self.quadb[:, 0, :], start=True, stop=True),
                    reads=(self.self4_r[0], self.quadb_r[0]), writes=(self.pb_r[4],))
                P.add("vector", lambda e, g=g: e.tensor_tensor(
                    out=qw[:, g, :], in0=qw[:, g, :], in1=self.pb[4][:], op=ALU.mult),
                    reads=(qw_r[0], self.pb_r[4]), writes=(qw_r[0],))
            for h in range(4):
                pend_o = None

                def oacc(h2, mb, sl2):
                    for ob in range(2):
                        bo = 7 if ob else 4
                        P.add("tensor", lambda e, h2=h2, mb=mb, ob=ob, bo=bo, sl2=sl2: e.matmul(
                            out=self.pb[bo][:], lhsT=w1_b[:, h2, mb, ob * 128:(ob + 1) * 128],
                            rhs=zst[:, sl2, :], start=(mb == 0), stop=(mb == 3)),
                            reads=(w1_r[0], zst_r[sl2]), writes=(self.pb_r[bo],))

                for mb in range(4):
                    sl2 = mb % 2
                    bg, bh2 = (5, 6) if mb % 2 == 0 else (0, 1)
                    P.add("tensor", lambda e, h=h, mb=mb, bg=bg: [
                        e.matmul(out=self.pb[bg][:], lhsT=w0_b[:, h, kg, mb * 128:(mb + 1) * 128],
                                 rhs=qw[:, 2 * h + kg, :], start=(kg == 0), stop=(kg == 1))
                        for kg in range(2)][-1],
                        reads=(w0_r[0], qw_r[0]), writes=(self.pb_r[bg],))
                    P.add("tensor", lambda e, h=h, mb=mb, bh2=bh2: [
                        e.matmul(out=self.pb[bh2][:], lhsT=w2_b[:, h, kg, mb * 128:(mb + 1) * 128],
                                 rhs=qw[:, 2 * h + kg, :], start=(kg == 0), stop=(kg == 1))
                        for kg in range(2)][-1],
                        reads=(w2_r[0], qw_r[0]), writes=(self.pb_r[bh2],))
                    P.add("scalar", lambda e, sl2=sl2, bg=bg: e.activation(
                        out=sg[:, sl2, :], in_=self.pb[bg][:], func=AF.Silu),
                        reads=(self.pb_r[bg],), writes=(sg_r[sl2],))
                    P.add("vector", lambda e, sl2=sl2, bh2=bh2: e.tensor_tensor(
                        out=zst[:, sl2, :], in0=sg[:, sl2, :], in1=self.pb[bh2][:], op=ALU.mult),
                        reads=(sg_r[sl2], self.pb_r[bh2]), writes=(zst_r[sl2],))
                    if pend_o is not None:
                        oacc(*pend_o)
                    pend_o = (h, mb, sl2)
                oacc(*pend_o)
                for ob in range(2):
                    bo = 7 if ob else 4
                    P.add("vector", lambda e, h=h, ob=ob, cs=cs, bo=bo: e.tensor_tensor(
                        out=self.xs[:, 2 * h + ob, cs], in0=self.xs[:, 2 * h + ob, cs],
                        in1=self.pb[bo][:], op=ALU.add),
                        reads=(self.xs_r[2 * h + ob][ch], self.pb_r[bo]), writes=(self.xs_r[2 * h + ob][ch],))

    def phase_moe(self):
        nc, P = self.nc, self.P
        wrap = self.dram_in("WrT", [16, 1024, 1024])
        wsap = self.dram_in("WsT", [2, 1024, 1024])
        wr_b, wr_rs = self.tile("wr_b", [128, 2, G, 1024], BF16, 2)
        self.stg, self.stg_r = self.tile("stg", [128, 2, 256], F32, 2)
        gT, gT_r = self.tile("gT", [16, T], BF16, NCH)
        self._gT, self._gT_r = gT, gT_r
        su, su_r = self.tile("su", [128, 2, G, 512], BF16, 2)
        gbs, gbs_r = self.tile("gbs", [128, 2, 512], BF16, 2)
        grow, grow_r = self.tile("grow", [1, 2, 512], BF16, 2)
        for ch in range(NCH):
            for tt in range(TC // 128):
                tile0 = ch * TC + tt * 128
                tsl = slice(tile0, tile0 + 128)
                bS = tt % 2
                bT = 2 + tt % 2
                s_i = tt % 4
                P.add("tensor", lambda e, tsl=tsl, bS=bS: [
                    e.matmul(out=self.pb[bS][0:128, 0:16], lhsT=self.xn[:, kg, tsl],
                             rhs=self.cent_b[:, kg, :], start=(kg == 0), stop=(kg == G - 1))
                    for kg in range(G)][-1],
                    reads=(self.cent_r[0],) + tuple(self.xn_r[kg][ch] for kg in range(G)), writes=(self.pb_r[bS],))
                def gate_math(e, bS=bS, s_i=s_i):
                    sco = self.sq16[:, s_i, :]
                    c0 = self.col1[:, 2 * s_i:2 * s_i + 1]
                    e.tensor_tensor(out=sco, in0=self.pb[bS][0:128, 0:16], in1=self.ebias_bc[:], op=ALU.add)
                    e.drain()
                    return e.tensor_reduce(out=c0, in_=sco, axis=AX.X, op=ALU.max, negate=True)
                P.add("vector", gate_math, reads=(self.pb_r[bS], self.ebias_r[0]),
                      writes=(self.sq16_r[s_i], self.col1_r[2 * s_i], self.col1_r[2 * s_i + 1]))
                P.add("scalar", lambda e, s_i=s_i: e.activation(
                    out=self.sq16[:, s_i, :], in_=self.sq16[:, s_i, :], func=AF.Exp,
                    bias=self.col1[:, 2 * s_i:2 * s_i + 1], scale=1.0),
                    reads=(self.sq16_r[s_i], self.col1_r[2 * s_i]), writes=(self.sq16_r[s_i],))
                def gate_math2(e, s_i=s_i):
                    sco = self.sq16[:, s_i, :]
                    c0 = self.col1[:, 2 * s_i:2 * s_i + 1]
                    c1 = self.col1[:, 2 * s_i + 1:2 * s_i + 2]
                    e.tensor_reduce(out=c0, in_=sco, axis=AX.X, op=ALU.add)
                    e.drain()
                    e.reciprocal(out=c0, in_=c0)
                    e.drain()
                    e.tensor_scalar(out=sco, in0=sco, scalar1=c0, scalar2=None, op0=ALU.mult)
                    e.drain()
                    e.tensor_reduce(out=c0, in_=sco, axis=AX.X, op=ALU.max)
                    e.drain()
                    e.tensor_scalar(out=self.fstg[0:128, 0, 0:16], in0=sco, scalar1=c0,
                                    scalar2=-1e30, op0=ALU.is_ge, op1=ALU.mult)
                    e.drain()
                    e.tensor_tensor(out=self.fstg[0:128, 0, 0:16], in0=self.fstg[0:128, 0, 0:16],
                                    in1=sco, op=ALU.add)
                    e.drain()
                    e.tensor_reduce(out=c1, in_=self.fstg[0:128, 0, 0:16], axis=AX.X, op=ALU.max)
                    e.drain()
                    e.tensor_scalar(out=self.fstg[0:128, 0, 0:16], in0=sco, scalar1=c1,
                                    scalar2=None, op0=ALU.is_ge)
                    e.drain()
                    return e.tensor_tensor(out=sco, in0=sco, in1=self.fstg[0:128, 0, 0:16], op=ALU.mult)
                P.add("vector", gate_math2, reads=(self.sq16_r[s_i],),
                      writes=(self.sq16_r[s_i], self.col1_r[2 * s_i], self.col1_r[2 * s_i + 1],
                              self.fstg_r[0]))
                P.add("tensor", lambda e, s_i=s_i, bT=bT: e.matmul(
                    out=self.pb[bT][0:16, 0:128], lhsT=self.sq16[:, s_i, :], rhs=self.ident[:],
                    start=True, stop=True, is_transpose=True),
                    reads=(self.sq16_r[s_i], self.ident_r[0]), writes=(self.pb_r[bT],))
                P.add("scalar", lambda e, bT=bT, tsl=tsl: e.copy(
                    out=gT[:, tsl], in_=self.pb[bT][0:16, 0:128]),
                    reads=(self.pb_r[bT],), writes=(gT_r[ch],))
        for p in range(9):
            shared = (p == 8)
            if not shared:
                pair = wrap[2 * p:2 * p + 2].rearrange("i (kg p) m -> p i kg m", p=128)
                P.add("gpsimd", lambda e, pair=pair: e.dma_start(out=wr_b[:], in_=pair),
                      writes=(wr_rs[0], wr_rs[1]), dma=True)
            else:
                for i in range(2):
                    self.load_w(wsap[i], wr_b[:, i, :, :], wr_rs[i], G, 1024, scale=0.5)
            for ch in range(NCH):
                c0 = ch * TC
                cs = slice(c0, c0 + TC)
                if not shared:
                    for i in range(2):
                        eidx = 2 * p + i
                        P.add("scalar", lambda e, eidx=eidx, cs=cs, i=i: e.dma_start(
                            out=grow[:, i, :], in_=gT[eidx:eidx + 1, cs]),
                            reads=(gT_r[ch],), writes=(grow_r[i],), dma=True)
                        P.add("tensor", lambda e, i=i: e.matmul(
                            out=self.pb[6][:], lhsT=self.ones_row_bf[:], rhs=grow[:, i, :],
                            start=True, stop=True),
                            reads=(grow_r[i], self.orb_r[0]), writes=(self.pb_r[6],))
                        P.add("vector", lambda e, i=i: e.tensor_copy(out=gbs[:, i, :], in_=self.pb[6][:]),
                              reads=(self.pb_r[6],), writes=(gbs_r[i],))
                for i in range(2):
                    eidx = 2 * p + i
                    if not shared:
                        for kg in range(G):
                            P.add("vector", lambda e, kg=kg, cs=cs, i=i: e.tensor_tensor(
                                out=su[:, i, kg, :], in0=self.xn[:, kg, cs], in1=gbs[:, i, :], op=ALU.mult),
                                reads=(self.xn_r[kg][ch], gbs_r[i]), writes=(su_r[i],))
                        rd = lambda kg, cs, i=i: su[:, i, kg, :]
                    else:
                        rd = lambda kg, cs: self.xn[:, kg, cs]
                        rhs_res = None
                    for ob in range(8):
                        first = (i == 0)
                        last = (i == 1)
                        rres = (su_r[i],) if not shared else tuple(self.xn_r[kg][ch] for kg in range(G))
                        P.add("tensor", lambda e, i=i, ob=ob, cs=cs, rd=rd, first=first, last=last: [
                            e.matmul(out=self.pb[ob][:], lhsT=wr_b[:, i, kg, ob * 128:(ob + 1) * 128],
                                     rhs=rd(kg, cs), start=(first and kg == 0), stop=(last and kg == G - 1))
                            for kg in range(G)][-1],
                            reads=(wr_rs[i],) + rres, writes=(self.pb_r[ob],))
                if shared:
                    for ob in range(8):
                        P.add("tensor", lambda e, ob=ob, cs=cs: e.matmul(
                            out=self.pb[ob][:], lhsT=self.br_b[:, ob * 128:(ob + 1) * 128],
                            rhs=gT[:, cs], start=False, stop=False, skip_group_check=True),
                            reads=(self.br_r[0], gT_r[ch]), writes=(self.pb_r[ob],))
                for ob in range(8):
                    if not shared:
                        P.add("vector", lambda e, ob=ob, cs=cs: e.tensor_tensor(
                            out=self.xs[:, ob, cs], in0=self.xs[:, ob, cs], in1=self.pb[ob][:], op=ALU.add),
                            reads=(self.xs_r[ob][ch], self.pb_r[ob]), writes=(self.xs_r[ob][ch],))
                    else:
                        fs = ob % 2
                        P.add("vector", lambda e, ob=ob, cs=cs, fs=fs: e.scalar_tensor_tensor(
                            out=self.fstg[:, fs, :], in0=self.pb[ob][:], scalar=self.bsh[:, ob:ob + 1],
                            in1=self.xn[:, ob, cs], op0=ALU.add, op1=ALU.add),
                            reads=(self.pb_r[ob], self.bsh_r[0], self.xn_r[ob][ch]), writes=(self.fstg_r[fs],))
                        P.add("vector", lambda e, ob=ob, cs=cs, fs=fs: e.tensor_tensor(
                            out=self.xs[:, ob, cs], in0=self.xs[:, ob, cs], in1=self.fstg[:, fs, :], op=ALU.add),
                            reads=(self.xs_r[ob][ch], self.fstg_r[fs]), writes=(self.xs_r[ob][ch],))

    def phase_ffn1(self):
        nc, P = self.nc, self.P
        wfap = self.dram_in("wfT", [1024, 1024])
        wfstg, wfstg_r = self.tile("wfstg", [128, 2, 1024], F32, 2)
        tern, tern_r = self.tile("tern", [128, G, 1024], BF16, 1)
        wfabs, wfabs_r = self.tile("wfabs", [128, 9], F32)
        dcol, dcol_r = self.tile("dcol", [128, 2], F32)
        for kg in range(G):
            sl = kg % 2
            P.add("sync", lambda e, kg=kg, sl=sl: e.dma_start(
                out=wfstg[:, sl, :], in_=wfap[kg * 128:(kg + 1) * 128, :]),
                writes=(wfstg_r[sl],), dma=True)
            P.add("vector", lambda e, kg=kg, sl=sl: e.tensor_reduce(
                out=wfabs[:, kg:kg + 1], in_=wfstg[:, sl, :], axis=AX.X, op=ALU.add,
                apply_absolute_value=True),
                reads=(wfstg_r[sl],), writes=(wfabs_r[0],))
        P.add("vector", lambda e: e.tensor_reduce(
            out=wfabs[:, 8:9], in_=wfabs[:, 0:8], axis=AX.X, op=ALU.add),
            reads=(wfabs_r[0],), writes=(wfabs_r[0],))
        P.add("tensor", lambda e: e.matmul(
            out=self.pb[0][0:1, 0:1], lhsT=self.ones_col_f32[:], rhs=wfabs[:, 8:9],
            start=True, stop=True),
            reads=(wfabs_r[0], self.ocf_r[0]), writes=(self.pb_r[0],))
        P.add("vector", lambda e: e.tensor_scalar(
            out=wfabs[:, 8:9].rearrange("p c -> p c") if False else self.srow[:, 0, 0:1],
            in0=self.pb[0][0:1, 0:1], scalar1=0.7 / (1024.0 * 1024.0), scalar2=None, op0=ALU.mult),
            reads=(self.pb_r[0],), writes=(self.srow_r[0],))
        P.add("tensor", lambda e: e.matmul(
            out=self.pb[1][0:128, 0:1], lhsT=self.ones_row_f32[:], rhs=self.srow[:, 0, 0:1],
            start=True, stop=True),
            reads=(self.srow_r[0], self.orf_r[0]), writes=(self.pb_r[1],))
        def dbuild(e):
            e.tensor_copy(out=dcol[:, 0:1], in_=self.pb[1][0:128, 0:1])
            e.drain()
            return e.tensor_scalar(out=dcol[:, 1:2], in0=dcol[:, 0:1], scalar1=-1.0, scalar2=None, op0=ALU.mult)
        P.add("vector", dbuild, reads=(self.pb_r[1],), writes=(dcol_r[0],))
        for kg in range(G):
            sl = kg % 2
            P.add("sync", lambda e, kg=kg, sl=sl: e.dma_start(
                out=wfstg[:, sl, :], in_=wfap[kg * 128:(kg + 1) * 128, :]),
                writes=(wfstg_r[sl],), dma=True)
            def tern_build(e, kg=kg, sl=sl):
                e.tensor_scalar(out=tern[:, kg, :],
                                in0=wfstg[:, sl, :], scalar1=dcol[:, 0:1], scalar2=None, op0=ALU.is_gt)
                e.drain()
                e.tensor_scalar(out=wfstg[:, sl, :], in0=wfstg[:, sl, :], scalar1=dcol[:, 1:2],
                                scalar2=None, op0=ALU.is_lt)
                e.drain()
                return e.tensor_tensor(out=tern[:, kg, :], in0=tern[:, kg, :], in1=wfstg[:, sl, :],
                                       op=ALU.subtract)
            P.add("vector", tern_build, reads=(wfstg_r[sl], dcol_r[0]),
                  writes=(tern_r[0], wfstg_r[sl]))
        for ch in range(NCH):
            c0 = ch * TC
            cs = slice(c0, c0 + TC)
            for ob in range(8):
                bk = ob % 4 + 2
                P.add("tensor", lambda e, ob=ob, cs=cs, bk=bk: [
                    e.matmul(out=self.pb[bk][:], lhsT=tern[:, kg, ob * 128:(ob + 1) * 128],
                             rhs=self.xn[:, kg, cs], start=(kg == 0), stop=(kg == G - 1))
                    for kg in range(G)][-1],
                    reads=(tern_r[0],) + tuple(self.xn_r[kg][ch] for kg in range(G)), writes=(self.pb_r[bk],))
                fs = ob % 2
                P.add("vector", lambda e, ob=ob, bk=bk, fs=fs: e.tensor_scalar(
                    out=self.fstg[:, fs, :], in0=self.pb[bk][:], scalar1=self.alph_bc[:, 0:1],
                    scalar2=self.bias_pp[:, ob:ob + 1], op0=ALU.mult, op1=ALU.add),
                    reads=(self.pb_r[bk], self.alphbc_r[0], self.bias_r[0]), writes=(self.fstg_r[fs],))
                P.add("vector", lambda e, ob=ob, cs=cs, fs=fs: e.tensor_tensor(
                    out=self.xs[:, ob, cs], in0=self.xs[:, ob, cs], in1=self.fstg[:, fs, :], op=ALU.add),
                    reads=(self.xs_r[ob][ch], self.fstg_r[fs]), writes=(self.xs_r[ob][ch],))

    def phase_rms(self):
        P = self.P
        sq, sq_r = self.tile(f"rmssq{self.uid}", [128, 2, 512], BF16, 2)
        self.uid += 1
        for ch in range(NCH):
            par = ch % 2
            bA, bR = 4 * par, 4 * par + 2
            cs = slice(ch * TC, (ch + 1) * TC)
            for g in range(G):
                P.add("scalar", lambda e, g=g, cs=cs, par=par: e.square(
                    out=sq[:, par, :], in_=self.xn[:, g, cs]),
                    reads=(self.xn_r[g][ch],), writes=(sq_r[par],))
                P.add("tensor", lambda e, g=g, par=par, bA=bA: e.matmul(
                    out=self.pb[bA][0:1, :], lhsT=self.ones_col_bf[:, 0:1],
                    rhs=sq[:, par, :], start=(g == 0), stop=(g == G - 1)),
                    reads=(self.ocb_r[0], sq_r[par]), writes=(self.pb_r[bA],))
            sm = 0
            n_s, r_s = sm, sm + 1
            P.add("scalar", lambda e, bA=bA, n_s=n_s: e.activation(
                out=self.srow[:, n_s, :], in_=self.pb[bA][0:1, :], func=AF.Sqrt),
                reads=(self.pb_r[bA],), writes=(self.srow_r[n_s],))
            def rchain(e, n_s=n_s, r_s=r_s):
                e.tensor_scalar(out=self.srow[:, n_s, :], in0=self.srow[:, n_s, :],
                                scalar1=1e-12, scalar2=None, op0=ALU.max)
                e.drain()
                return e.reciprocal(out=self.srow[:, r_s, :], in_=self.srow[:, n_s, :])
            P.add("vector", rchain, reads=(self.srow_r[n_s],), writes=(self.srow_r[n_s], self.srow_r[r_s]))
            P.add("tensor", lambda e, r_s=r_s, bR=bR: e.matmul(
                out=self.pb[bR][:], lhsT=self.ones_row_f32[:], rhs=self.srow[:, r_s, :],
                start=True, stop=True),
                reads=(self.orf_r[0], self.srow_r[r_s]), writes=(self.pb_r[bR],))
            for g in range(G):
                P.add("vector", lambda e, g=g, cs=cs, bR=bR: e.scalar_tensor_tensor(
                    out=self.xn[:, g, cs], in0=self.xn[:, g, cs], scalar=self.gp[:, g:g + 1],
                    in1=self.pb[bR][:], op0=ALU.mult, op1=ALU.mult),
                    reads=(self.xn_r[g][ch], self.gp_r[0], self.pb_r[bR]), writes=(self.xn_r[g][ch],))

    def phase_attn1(self):
        nc, P = self.nc, self.P
        wqkvap = self.dram_in("wqkv", [1024, 3072])
        woap = self.dram_in("wout", [1024, 1024])
        wqk_b, wqk_r = self.tile("wqk_b", [128, G, 512], BF16, 1)
        wv_b, wv_r = self.tile("wv_b", [128, G, 256], BF16, 1)
        wo_b, wo_r = self.tile("wo_b", [128, 2, 1024], BF16, 1)
        kaug, kaug_r = self.tile("kaug", [65, 4, 2048], BF16, 4)
        qaug, qaug_r = self.tile("qaug", [65, 4, 512], BF16, 4)
        vaug, vaug_r = self.tile("vaug", [128, 16, 4, 65], BF16, 16)
        sexp, sexp_r = self.tile("sexp", [128, 2, 1024], BF16, 2)
        sqst, sqst_r = self.tile("sqst", [64, 2, 512], BF16, 2)
        maxk, maxk_r = self.tile("maxk", [1, 8], F32, 5)
        rbrow, rb_r = self.tile("rbrow", [1, 1, 512], F32, 1)
        rbs, rbs_r = self.tile("rbs", [64, 1, 512], BF16, 1)
        ohb, ohb_r = self.tile("ohb", [128, 2, 4, 512], BF16, 8)

        def ohres(h, ic):
            return ohb_r[(h // 2) * 4 + ic]

        for b in range(2):
            for hg in range(4):
                h0 = 4 * hg
                qk = wqkvap[:, 64 * h0:64 * h0 + 256]
                kk = wqkvap[:, 1024 + 64 * h0:1024 + 64 * h0 + 256]
                vv = wqkvap[:, 2048 + 64 * h0:2048 + 64 * h0 + 256]
                for (colap, dst0) in ((qk, 0), (kk, 256)):
                    P.add("gpsimd", lambda e, colap=colap, dst0=dst0: e.dma_start(
                        out=wqk_b[:, :, dst0:dst0 + 256],
                        in_=colap.rearrange("(kg p) m -> p kg m", p=128)),
                        writes=(wqk_r[0],), dma=True)
                P.add("gpsimd", lambda e, vv=vv: e.dma_start(
                    out=wv_b[:], in_=vv.rearrange("(kg p) m -> p kg m", p=128)),
                    writes=(wv_r[0],), dma=True)
                self.load_w(woap[256 * hg:256 * (hg + 1), :], wo_b, wo_r[0], 2, 1024)
                t0 = b * 2048
                P.add("vector", lambda e: e.memset(kaug[64:65, :, :], 1.0),
                      writes=tuple(kaug_r))
                P.add("vector", lambda e: e.memset(vaug[:, :, :, 64:65], 1.0),
                      writes=tuple(vaug_r))
                for jc in range(4):
                    cs = slice(t0 + jc * TC, t0 + (jc + 1) * TC)
                    jsl = slice(jc * TC, (jc + 1) * TC)
                    xnr = tuple(self.xn_r[kg][4 * b + jc] for kg in range(G))
                    pend_k = None

                    def kstats(h, sl, jc=jc):
                        P.add("tensor", lambda e, sl=sl: e.matmul(
                            out=self.pb[2][0:1, :], lhsT=self.ones_col_bf[0:64, 0:1],
                            rhs=sqst[:, sl, :], start=True, stop=True),
                            reads=(sqst_r[sl], self.ocb_r[0]), writes=(self.pb_r[2],))
                        P.add("scalar", lambda e: e.activation(
                            out=self.srow[:, 3, :], in_=self.pb[2][0:1, :], func=AF.Sqrt),
                            reads=(self.pb_r[2],), writes=(self.srow_r[3],))
                        P.add("vector", lambda e: e.tensor_reduce(
                            out=maxk[:, 4:5], in_=self.srow[:, 3, :], axis=AX.X, op=ALU.max),
                            reads=(self.srow_r[3],), writes=(maxk_r[4],))
                        if jc == 0:
                            P.add("vector", lambda e, h=h: e.tensor_copy(
                                out=maxk[:, h:h + 1], in_=maxk[:, 4:5]),
                                reads=(maxk_r[4],), writes=(maxk_r[h],))
                        else:
                            P.add("vector", lambda e, h=h: e.tensor_tensor(
                                out=maxk[:, h:h + 1], in0=maxk[:, h:h + 1], in1=maxk[:, 4:5], op=ALU.max),
                                reads=(maxk_r[4], maxk_r[h]), writes=(maxk_r[h],))

                    for h in range(4):
                        bK = h % 2
                        P.add("tensor", lambda e, h=h, cs=cs, bK=bK: [
                            e.matmul(out=self.pb[bK][0:64, :], lhsT=wqk_b[:, kg, 256 + 64 * h:256 + 64 * h + 64],
                                     rhs=self.xn[:, kg, cs], start=(kg == 0), stop=(kg == G - 1))
                            for kg in range(G)][-1],
                            reads=(wqk_r[0],) + xnr, writes=(self.pb_r[bK],))
                        P.add("scalar", lambda e, h=h, jsl=jsl, bK=bK: e.copy(
                            out=kaug[0:64, h, jsl], in_=self.pb[bK][0:64, :]),
                            reads=(self.pb_r[bK],), writes=(kaug_r[h],))
                        P.add("scalar", lambda e, bK=bK: e.square(
                            out=sqst[:, bK, :], in_=self.pb[bK][0:64, :]),
                            reads=(self.pb_r[bK],), writes=(sqst_r[bK],))
                        if pend_k is not None:
                            kstats(*pend_k)
                        pend_k = (h, h % 2)
                    kstats(*pend_k)
                    for tt2 in range(4):
                        jb = jc * 4 + tt2
                        tsl = slice(t0 + jb * 128, t0 + (jb + 1) * 128)
                        bV = 3 if tt2 % 2 == 0 else 4
                        P.add("tensor", lambda e, tsl=tsl, bV=bV: [
                            e.matmul(out=self.pb[bV][0:128, 0:256], lhsT=self.xn[:, kg, tsl],
                                     rhs=wv_b[:, kg, :], start=(kg == 0), stop=(kg == G - 1))
                            for kg in range(G)][-1],
                            reads=(wv_r[0],) + xnr, writes=(self.pb_r[bV],))
                        def vevac(e, jb=jb, bV=bV):
                            for h in range(4):
                                i = e.copy(out=vaug[:, jb, h, 0:64],
                                           in_=self.pb[bV][0:128, 64 * h:64 * h + 64])
                            return i
                        P.add("scalar", vevac, reads=(self.pb_r[bV],), writes=(vaug_r[jb],))
                for h in range(4):
                    if ATTN_LVL < 2:
                        break
                    pend_q = None

                    def qstats(ic, sl, h=h):
                        P.add("tensor", lambda e, sl=sl: e.matmul(
                            out=self.pb[2][0:1, :], lhsT=self.ones_col_bf[0:64, 0:1],
                            rhs=sqst[:, sl, :], start=True, stop=True),
                            reads=(sqst_r[sl], self.ocb_r[0]), writes=(self.pb_r[2],))
                        P.add("scalar", lambda e: e.activation(
                            out=self.srow[:, 3, :], in_=self.pb[2][0:1, :], func=AF.Sqrt),
                            reads=(self.pb_r[2],), writes=(self.srow_r[3],))
                        P.add("vector", lambda e, h=h, ic=ic: e.tensor_scalar(
                            out=self.srow[:, 2, :], in0=self.srow[:, 3, :],
                            scalar1=maxk[:, h:h + 1], scalar2=-1.0, op0=ALU.mult, op1=ALU.mult),
                            reads=(self.srow_r[3], maxk_r[h]), writes=(self.srow_r[2],))
                        P.add("scalar", lambda e, ic=ic: e.copy(
                            out=qaug[64:65, ic, :], in_=self.srow[:, 2, :]),
                            reads=(self.srow_r[2],), writes=(qaug_r[ic],))

                    for ic in range(4):
                        ics = slice(t0 + ic * TC, t0 + (ic + 1) * TC)
                        bQ = 3 if ic % 2 == 0 else 4
                        xnr = tuple(self.xn_r[kg][4 * b + ic] for kg in range(G))
                        P.add("tensor", lambda e, h=h, ics=ics, bQ=bQ: [
                            e.matmul(out=self.pb[bQ][0:64, :], lhsT=wqk_b[:, kg, 64 * h:64 * h + 64],
                                     rhs=self.xn[:, kg, ics], start=(kg == 0), stop=(kg == G - 1))
                            for kg in range(G)][-1],
                            reads=(wqk_r[0],) + xnr, writes=(self.pb_r[bQ],))
                        P.add("scalar", lambda e, ic=ic, bQ=bQ: e.activation(
                            out=qaug[0:64, ic, :], in_=self.pb[bQ][0:64, :], func=AF.Copy,
                            bias=0.0, scale=0.125),
                            reads=(self.pb_r[bQ],), writes=(qaug_r[ic],))
                        P.add("scalar", lambda e, ic=ic, bQ=bQ: e.activation(
                            out=sqst[:, ic % 2, :], in_=self.pb[bQ][0:64, :], func=AF.Square, scale=0.125),
                            reads=(self.pb_r[bQ],), writes=(sqst_r[ic % 2],))
                        if pend_q is not None:
                            qstats(*pend_q)
                        pend_q = (ic, ic % 2)
                    qstats(*pend_q)
                    def emit_norm(ic, bO, h=h):
                        P.add("scalar", lambda e, bO=bO, ic=ic: e.activation(
                            out=self.srow[:, 1, :], in_=self.pb[bO][64:65, :], func=AF.Ln,
                            scale=1099511627776.0),
                            reads=(self.pb_r[bO],), writes=(self.srow_r[1],))
                        P.add("scalar", lambda e, ic=ic: e.activation(
                            out=rbrow[:, 0, :], in_=self.srow[:, 1, :], func=AF.Exp,
                            bias=self.cbias[0:1, 2:3], scale=-1.0),
                            reads=(self.srow_r[1], self.cbias_r[0]), writes=(rb_r[0],))
                        P.add("tensor", lambda e, ic=ic: e.matmul(
                            out=self.pb[2][0:64, :], lhsT=self.ones_row_f32[0:1, 0:64],
                            rhs=rbrow[:, 0, :], start=True, stop=True),
                            reads=(rb_r[0], self.orf_r[0]), writes=(self.pb_r[2],))
                        P.add("scalar", lambda e, ic=ic: e.copy(
                            out=rbs[:, 0, :], in_=self.pb[2][0:64, :]),
                            reads=(self.pb_r[2],), writes=(rbs_r[0],))
                        P.add("vector", lambda e, h=h, ic=ic, bO=bO: e.tensor_tensor(
                            out=ohb[64 * (h % 2):64 * (h % 2) + 64, h // 2, ic, :],
                            in0=self.pb[bO][0:64, :], in1=rbs[:, 0, :], op=ALU.mult),
                            reads=(self.pb_r[bO], rbs_r[0]), writes=(ohres(h, ic),))

                    for icp in (0, 2):
                        if ATTN_LVL < 3:
                            break
                        def sim_pair(pr, s):
                            bb = 0 if s == 0 else 6
                            ic = icp + s
                            for half in range(2):
                                jb = 2 * pr + half
                                P.add("tensor", lambda e, h=h, jb=jb, bb=bb, half=half, ic=ic: e.matmul(
                                    out=self.pb[bb + half][:], lhsT=kaug[:, h, jb * 128:(jb + 1) * 128],
                                    rhs=qaug[:, ic, :], start=True, stop=True),
                                    reads=(kaug_r[h], qaug_r[ic]), writes=(self.pb_r[bb + half],))
                        sim_pair(0, 0)
                        sim_pair(0, 1)
                        for pr in range(8):
                            for s in range(2):
                                bb = 0 if s == 0 else 6
                                bO = 3 + s
                                P.add("scalar", lambda e, s=s, bb=bb: e.activation(
                                    out=sexp[:, s, :], in_=self.pbH[bb // 2][:, 0:1024], func=AF.Exp),
                                    reads=(self.pb_r[bb], self.pb_r[bb + 1]), writes=(sexp_r[s],))
                                if pr < 7:
                                    sim_pair(pr + 1, s)
                                for half in range(2):
                                    jb = 2 * pr + half
                                    P.add("tensor", lambda e, h=h, jb=jb, s=s, half=half, bO=bO: e.matmul(
                                        out=self.pb[bO][0:65, :], lhsT=vaug[:, jb, h, :],
                                        rhs=sexp[:, s, half * 512:half * 512 + 512],
                                        start=(jb == 0), stop=(jb == 15)),
                                        reads=(vaug_r[jb], sexp_r[s]), writes=(self.pb_r[bO],))
                        if ATTN_LVL < 4:
                            continue
                        emit_norm(icp, 3)
                        emit_norm(icp + 1, 4)
                for ic in range(4):
                    if ATTN_LVL < 5:
                        break
                    ics = slice(t0 + ic * TC, t0 + (ic + 1) * TC)
                    for ob in range(8):
                        bk = 2 if (ic * 8 + ob) % 2 == 0 else 5
                        P.add("tensor", lambda e, ob=ob, bk=bk, ic=ic: [
                            e.matmul(out=self.pb[bk][:], lhsT=wo_b[:, kg, ob * 128:(ob + 1) * 128],
                                     rhs=ohb[:, kg, ic, :], start=(kg == 0), stop=(kg == 1))
                            for kg in range(2)][-1],
                            reads=(wo_r[0], ohres(0, ic), ohres(2, ic)), writes=(self.pb_r[bk],))
                        if ATTN_LVL >= 6:
                            P.add("vector", lambda e, ob=ob, ics=ics, bk=bk, ic=ic: e.tensor_tensor(
                                out=self.xs[:, ob, ics], in0=self.xs[:, ob, ics], in1=self.pb[bk][:], op=ALU.add),
                                reads=(self.xs_r[ob][4 * b + ic], self.pb_r[bk]), writes=(self.xs_r[ob][4 * b + ic],))



def build_nc(stage):
    nc = bass.Bass("TRN2", target_bir_lowering=False)
    P = Prog()
    with contextlib.ExitStack() as st:
        kb = KB(nc, P, st)
        kb.alloc_global()
        if stage == "A":
            kb.load_params(("l0",))
            kb.load_x_fp32()
            kb.phase_ln()
            kb.phase_l0attn()
            kb.store_bf("x_out", kb.xs, kb.xs_r)
            kb.store_bf("xn_out", kb.xn, kb.xn_r)
        elif stage == "B":
            kb.load_params(("moe",))
            kb.load_bf("x_in", kb.xs, kb.xs_r)
            kb.load_bf("xn_in", kb.xn, kb.xn_r)
            kb.phase_moe()
            kb.store_bf("x_out", kb.xs, kb.xs_r)
            if getattr(build_nc, "debug_gates", False):
                gap = kb.dram_out("g_dbg", [16, T], BF16)
                gt = kb._gT
                for ch in range(NCH):
                    kb.P.add("gpsimd", lambda e, ch=ch: e.dma_start(
                        out=gap[:, ch * TC:(ch + 1) * TC], in_=gt[:, ch * TC:(ch + 1) * TC]),
                        reads=(kb._gT_r[ch],), dma=True)
        elif stage == "C":
            kb.load_params(("l1", "ffn1"))
            kb.load_bf("x_in", kb.xs, kb.xs_r)
            kb.phase_ln()
            kb.phase_ffn1()
            kb.phase_rms()
            kb.store_bf("x_out", kb.xs, kb.xs_r)
            kb.store_bf("xn_out", kb.xn, kb.xn_r)
        elif stage == "D":
            kb.load_params(("l1",))
            kb.load_bf("x_in", kb.xs, kb.xs_r)
            kb.load_bf("xn_in", kb.xn, kb.xn_r)
            kb.phase_attn1()
            kb.store_out_fp32()
        elif stage == "FUSED":
            kb.load_params(("l0", "moe", "l1", "ffn1"))
            kb.load_x_fp32()
            with kb.scope("left"):
                kb.phase_ln()
                kb.phase_l0attn()
            kb.P.fence_close("A")
            kb.P.fence_open("A")
            with kb.scope("right"):
                kb.phase_moe()
            kb.P.fence_close("B")
            kb.P.fence_open("B")
            with kb.scope("left"):
                kb.phase_ln()
                kb.phase_ffn1()
                kb.phase_rms()
            kb.P.fence_close("C")
            kb.P.fence_open("C")
            with kb.scope("right"):
                kb.phase_attn1()
            kb.store_out_fp32()
        P.emit(nc, st)
    return nc



def _prep_weights(inp):
    C = np.ascontiguousarray
    w = {}
    _dead = (inp["L0_attn_lr_proj_w"], inp["L0_attn_lr_proj_b"], inp["L0_attn_k_norm_w"])
    w["wqT"] = C(inp["L0_attn_qkv_w"][0:1024].T)
    w["qnw_pp"] = C(inp["L0_attn_q_norm_w"].reshape(8, 128).T)
    w["w0T"] = C(inp["L0_attn_w0"].transpose(0, 2, 1))
    w["w1T"] = C(inp["L0_attn_w1"].transpose(0, 2, 1))
    w["w2T"] = C(inp["L0_attn_w2"].transpose(0, 2, 1))
    w["centT"] = C(inp["L0_ffn_centroids"].T)
    w["ebias"] = C(inp["L0_ffn_expert_biases"].reshape(1, 16))
    w["WrT"] = C(inp["L0_ffn_Wr"].transpose(0, 2, 1))
    w["brr"] = C(inp["L0_ffn_br"])
    w["WsT"] = C(inp["L0_ffn_Ws"].transpose(0, 2, 1))
    w["bs_pp"] = C(inp["L0_ffn_bs"].reshape(2, 8, 128).transpose(0, 2, 1))
    w["gamma_pp"] = C(inp["L1_attn_gamma"].reshape(8, 128).T)
    w["wqkv"] = C(inp["L1_attn_w_qkv"])
    w["wout"] = C(inp["L1_attn_w_out"])
    w["wfT"] = C(inp["L1_ffn_weight"].T)
    w["alpha"] = C(inp["L1_ffn_alpha"].reshape(1, 1))
    w["bias_pp"] = C(inp["L1_ffn_bias"].reshape(8, 128).T)
    return w

_REQ = {
    "A": ["xT", "wqT", "qnw_pp", "w0T", "w1T", "w2T"],
    "B": ["x_in", "xn_in", "centT", "ebias", "WrT", "brr", "WsT", "bs_pp"],
    "C": ["x_in", "gamma_pp", "wfT", "alpha", "bias_pp"],
    "D": ["x_in", "xn_in", "gamma_pp", "wqkv", "wout"],
    "FUSED": ["xT", "wqT", "qnw_pp", "w0T", "w1T", "w2T", "centT", "ebias", "WrT", "brr",
              "WsT", "bs_pp", "gamma_pp", "wqkv", "wout", "wfT", "alpha", "bias_pp"],
}


STAGED = False

def kernel(**inp):
    w = _prep_weights(inp)
    x = inp["L0_attn_hidden_states"]
    xT = [np.ascontiguousarray(x[2 * c:2 * c + 2].reshape(T, 1024).T) for c in range(NCORES)]

    state = {c: {"xT": xT[c]} for c in range(NCORES)}
    stages = ("A", "B", "C", "D") if STAGED else ("FUSED",)
    for stage in stages:
        nc = build_nc(stage)
        in_maps = []
        for c in range(NCORES):
            m = {}
            for k in _REQ[stage]:
                m[k] = state[c][k] if k in ("xT", "x_in", "xn_in") else w[k]
            in_maps.append(m)
        res = run_bass_kernel_spmd(nc, in_maps, core_ids=list(range(NCORES)))
        for c in range(NCORES):
            r = res.results[c]
            if "x_out" in r:
                state[c]["x_in"] = r["x_out"]
            if "xn_out" in r:
                state[c]["xn_in"] = r["xn_out"]
            if "out" in r:
                state[c]["out"] = r["out"]

    out = np.empty((16, 2048, 1024), np.float32)
    for c in range(NCORES):
        out[2 * c:2 * c + 2] = np.asarray(state[c]["out"], np.float32).T.reshape(2, 2048, 1024)
    return out




# version 27 (final)
# speedup vs baseline: 1.0222x; 1.0222x over previous
import numpy as np
import contextlib
import concourse.bass as bass
import concourse.mybir as mybir
from concourse.bass_utils import run_bass_kernel_spmd

F32 = mybir.dt.float32
BF16 = mybir.dt.bfloat16
AF = mybir.ActivationFunctionType
ALU = mybir.AluOpType
AX = mybir.AxisListType

NCORES = 8
T = 4096
G = 8
TC = 512
NCH = T // TC

ENGS = ("tensor", "vector", "scalar", "gpsimd", "sync")
NLANES = 6
ATTN_LVL = 6


class Res:
    __slots__ = ("name", "lw", "readers")
    def __init__(self, name="r"):
        self.name = name
        self.lw = None
        self.readers = []

class _Op:
    __slots__ = ("idx", "eng", "fn", "dma", "lane", "wait_val", "throttle",
                 "drain_need", "waits", "marked", "rank", "eseq")
    def __init__(self, idx, eng, fn, dma):
        self.idx = idx
        self.eng = eng
        self.fn = fn
        self.dma = dma
        self.lane = 0
        self.wait_val = 0
        self.throttle = 0
        self.drain_need = -1
        self.waits = []
        self.marked = False
        self.rank = 0
        self.eseq = 0

class Prog:
    def __init__(self):
        self.ops = []
        self.eops = {e: [] for e in ENGS}
        self.dmacount = {e: 0 for e in ENGS}

    def add(self, eng, fn, reads=(), writes=(), dma=False):
        op = _Op(len(self.ops), eng, fn, dma)
        op.eseq = len(self.eops[eng])
        if dma:
            n = self.dmacount[eng]
            self.dmacount[eng] = n + 1
            op.lane = n % NLANES
            uses = n // NLANES
            op.throttle = 16 * uses
            op.wait_val = 16 * (uses + 1)
        deps = {}
        for r in reads:
            if r.lw is not None:
                deps[r.lw.idx] = r.lw
        for r in writes:
            if r.lw is not None:
                deps[r.lw.idx] = r.lw
            for rd in r.readers:
                deps[rd.idx] = rd
        for pidx in sorted(deps):
            p = deps[pidx]
            if p is op:
                continue
            if p.dma:
                op.waits.append(("dma", p.eng, p.lane, p.wait_val))
            elif p.eng == eng:
                if eng == "tensor" and not dma:
                    pass
                else:
                    if p.eseq > op.drain_need:
                        op.drain_need = p.eseq
            else:
                p.marked = True
                op.waits.append(("eng", p.eng, p))
        for r in writes:
            r.lw = op
            r.readers = []
        for r in reads:
            r.readers.append(op)
        self.ops.append(op)
        self.eops[eng].append(op)
        return op

    @staticmethod
    def _lane_totals(n):
        return [16 * ((n - l + NLANES - 1) // NLANES) if n > l else 0 for l in range(NLANES)]

    def fence_close(self, key):
        bars = [Res(f"bar_{key}_{e}") for e in ENGS]
        dmasnap = dict(self.dmacount)
        for e in ENGS:
            snap = dmasnap[e]
            def fnop(eng, e=e, snap=snap, prog=self):
                for l, v in enumerate(prog._lane_totals(snap)):
                    if v:
                        eng.wait_ge(prog._dma_sems[e][l], v)
                eng.drain()
                return eng.nop()
            self.add(e, fnop, reads=(), writes=(bars[ENGS.index(e)],))
        self._fences = getattr(self, "_fences", {})
        self._fences[key] = tuple(bars)

    def fence_open(self, key):
        allb = self._fences[key]
        for e in ENGS:
            self.add(e, lambda eng: eng.nop(), reads=allb, writes=())

    def barrier(self):
        self.fence_close(f"bar{len(self.ops)}")
        self.fence_open(f"bar{len(self.ops) - 10}" if False else list(self._fences)[-1])

    def emit(self, nc, st):
        self._eng_sems = {e: st.enter_context(nc.semaphore(f"es_{e}")) for e in ENGS}
        self._dma_sems = {e: [st.enter_context(nc.semaphore(f"ds_{e}{l}"))
                              for l in range(NLANES)] for e in ENGS}
        for e in ENGS:
            r = 0
            for op in self.eops[e]:
                if op.marked:
                    r += 1
                    op.rank = r
        prog = self
        with nc.Block() as block:
            def make_body(e):
                def body(eng):
                    drained = -1
                    waited = {}
                    last_seq = -1
                    for op in prog.eops[e]:
                        if op.drain_need > drained:
                            eng.drain()
                            drained = last_seq
                        for w in op.waits:
                            if w[0] == "dma":
                                key = ("d", w[1], w[2]); v = w[3]
                                sem = prog._dma_sems[w[1]][w[2]]
                            else:
                                key = ("e", w[1]); v = w[2].rank
                                sem = prog._eng_sems[w[1]]
                            if waited.get(key, 0) < v:
                                eng.wait_ge(sem, v)
                                waited[key] = v
                        if op.dma and op.throttle:
                            key = ("d", e, op.lane)
                            if waited.get(key, 0) < op.throttle:
                                eng.wait_ge(prog._dma_sems[e][op.lane], op.throttle)
                                waited[key] = op.throttle
                        inst = op.fn(eng)
                        if op.dma:
                            inst.then_inc(prog._dma_sems[e][op.lane], 16)
                        elif op.marked:
                            inst.then_inc(prog._eng_sems[e], 1)
                        last_seq = op.eseq
                    if e == "gpsimd":
                        for q in ENGS:
                            for l, v in enumerate(prog._lane_totals(prog.dmacount[q])):
                                if v:
                                    eng.wait_ge(prog._dma_sems[q][l], v)
                return body
            block.tensor(make_body("tensor"))
            block.vector(make_body("vector"))
            block.scalar(make_body("scalar"))
            block.gpsimd(make_body("gpsimd"))
            block.sync(make_body("sync"))



class KB:
    def __init__(self, nc, P, st):
        self.nc = nc
        self.P = P
        self.st = st
        self.cur_st = st
        self.d = {}
        self.uid = 0

    def tile(self, name, shape, dtype, nres=1):
        h = self.cur_st.enter_context(self.nc.sbuf_tensor(
            name, list(shape), dtype, side=getattr(self, "cur_side", "left")))
        rs = [Res(f"{name}.{i}") for i in range(nres)]
        return h, rs

    @contextlib.contextmanager
    def scope(self, side="left"):
        prev = self.cur_st
        prev_side = getattr(self, "cur_side", "left")
        with contextlib.ExitStack() as st2:
            self.cur_st = st2
            self.cur_side = side
            try:
                yield
            finally:
                self.cur_st = prev
                self.cur_side = prev_side

    def dram_in(self, name, shape, dtype=F32):
        t = self.nc.dram_tensor(name, list(shape), dtype, kind="ExternalInput")
        self.d[name] = t.ap()
        return self.d[name]

    def dram_out(self, name, shape, dtype=F32):
        t = self.nc.dram_tensor(name, list(shape), dtype, kind="ExternalOutput")
        self.d[name] = t.ap()
        return self.d[name]

    def alloc_global(self):
        nc, P, st = self.nc, self.P, self.st
        self.xs, _ = self.tile("xs", [128, G, T], BF16)
        self.xn, _ = self.tile("xn", [128, G, T], BF16)
        self.xs_r = [[Res(f"xs.{g}.{c}") for c in range(NCH)] for g in range(G)]
        self.xn_r = [[Res(f"xn.{g}.{c}") for c in range(NCH)] for g in range(G)]
        self.pbH = [st.enter_context(nc.psum_tensor(f"pbH{j}", [128, 1024], F32)) for j in range(4)]
        self.pb = [self.pbH[i // 2][:, (i % 2) * 512:(i % 2) * 512 + 512] for i in range(8)]
        self.pb_r = [Res(f"pb{i}") for i in range(8)]
        self.ident, self.ident_r = self.tile("ident", [128, 128], F32)
        self.ones_col_bf, self.ocb_r = self.tile("ones_col_bf", [128, 4], BF16)
        self.ones_row_bf, self.orb_r = self.tile("ones_row_bf", [1, 128], BF16)
        self.ones_row_f32, self.orf_r = self.tile("ones_row_f32", [1, 128], F32)
        self.ones_col_f32, self.ocf_r = self.tile("ones_col_f32", [128, 1], F32)
        self.sel4, self.sel4_r = self.tile("sel4", [128, 16], BF16)
        self.self4, self.self4_r = self.tile("self4", [4, 512], BF16)
        self.fstg, self.fstg_r = self.tile("fstg", [128, 2, 512], F32, 2)
        self.srow, self.srow_r = self.tile("srow", [1, 4, 512], F32, 4)
        self.col1, self.col1_r = self.tile("col1", [128, 16], F32, 16)
        self.sq16, self.sq16_r = self.tile("sq16", [128, 4, 16], F32, 4)
        self.cbias, self.cbias_r = self.tile("cbias", [128, 3], F32)

        P.add("gpsimd", lambda e: e.iota(self.fstg[:, 0, 0:128], [[1, 128]], channel_multiplier=0,
                                         allow_small_or_imprecise_dtypes=True),
              writes=(self.fstg_r[0],))
        P.add("gpsimd", lambda e: e.iota(self.fstg[:, 1, 0:128], [[0, 128]], channel_multiplier=1,
                                         allow_small_or_imprecise_dtypes=True),
              writes=(self.fstg_r[1],))
        P.add("vector", lambda e: e.tensor_tensor(out=self.ident[:], in0=self.fstg[:, 0, 0:128],
                                                  in1=self.fstg[:, 1, 0:128], op=ALU.is_equal),
              reads=(self.fstg_r[0], self.fstg_r[1]), writes=(self.ident_r[0],))
        P.add("vector", lambda e: e.memset(self.ones_col_bf[:], 1.0), writes=(self.ocb_r[0],))
        P.add("vector", lambda e: e.memset(self.ones_row_bf[:], 1.0), writes=(self.orb_r[0],))
        P.add("vector", lambda e: e.memset(self.ones_row_f32[:], 1.0), writes=(self.orf_r[0],))
        P.add("vector", lambda e: e.memset(self.ones_col_f32[:], 1.0), writes=(self.ocf_r[0],))
        def cb_fill(e):
            e.memset(self.cbias[:, 0:1], 1e-5)
            e.memset(self.cbias[:, 1:2], 1e-6)
            return e.memset(self.cbias[:, 2:3], 27.7258872223978)
        P.add("vector", cb_fill, writes=(self.cbias_r[0],))
        def sel4_build(e):
            i = e.memset(self.sel4[:], 0.0)
            for h in range(4):
                for g in (2 * h, 2 * h + 1):
                    i = e.memset(self.sel4[128 * 0:128, 4 * h + g // 2:4 * h + g // 2 + 1], 0.0)
            return i
        def sel4_fill(e):
            e.memset(self.sel4[:], 0.0)
            e.drain()
            for h in range(4):
                i = e.memset(self.sel4[:, 4 * h + h:4 * h + h + 1], 1.0)
            return i
        P.add("vector", sel4_fill, writes=(self.sel4_r[0],))
        P.add("gpsimd", lambda e: e.iota(self.fstg[0:4, 0, 0:512], [[1, 4], [0, 128]],
                                         channel_multiplier=0, allow_small_or_imprecise_dtypes=True),
              writes=(self.fstg_r[0],))
        P.add("gpsimd", lambda e: e.iota(self.fstg[0:4, 1, 0:512], [[0, 4], [0, 128]],
                                         channel_multiplier=1, allow_small_or_imprecise_dtypes=True),
              writes=(self.fstg_r[1],))
        P.add("vector", lambda e: e.tensor_tensor(out=self.self4[:], in0=self.fstg[0:4, 0, 0:512],
                                                  in1=self.fstg[0:4, 1, 0:512], op=ALU.is_equal),
              reads=(self.fstg_r[0], self.fstg_r[1]), writes=(self.self4_r[0],))

    def load_w(self, dram2d, dest, dest_res, kgs, M, dest_kg0=0, scale=None, tag=""):
        P = self.P
        if scale is None:
            src = dram2d[0:kgs * 128, 0:M].rearrange("(kg p) m -> p kg m", p=128)
            P.add("gpsimd", lambda e, src=src: e.dma_start(
                out=dest[:, dest_kg0:dest_kg0 + kgs, 0:M], in_=src),
                writes=(dest_res,), dma=True)
            return
        nchunk = (M + 255) // 256
        si = 0
        for kg in range(kgs):
            for cc in range(nchunk):
                c0 = cc * 256
                c1 = min(M, c0 + 256)
                slot = si % 2
                si += 1
                src = dram2d[kg * 128:(kg + 1) * 128, c0:c1]
                P.add("sync", lambda e, src=src, slot=slot, w=c1 - c0:
                      e.dma_start(out=self.stg[:, slot, 0:w], in_=src),
                      writes=(self.stg_r[slot],), dma=True)
                P.add("gpsimd", lambda e, kg=kg, c0=c0, c1=c1, slot=slot:
                      e.tensor_scalar(out=dest[:, dest_kg0 + kg, c0:c1],
                                      in0=self.stg[:, slot, 0:c1 - c0],
                                      scalar1=scale, scalar2=None, op0=ALU.mult),
                      reads=(self.stg_r[slot],), writes=(dest_res,))

    def load_params(self, need):
        nc, P = self.nc, self.P
        if "l0" in need:
            self.qnw, self.qnw_r = self.tile("qnw", [128, 8], F32)
            ap = self.dram_in("qnw_pp", [128, 8])
            P.add("sync", lambda e: e.dma_start(out=self.qnw[:], in_=ap[:]),
                  writes=(self.qnw_r[0],), dma=True)
        if "moe" in need:
            self.cent_b, self.cent_r = self.tile("cent_b", [128, G, 16], BF16)
            capf = self.dram_in("centT", [1024, 16])
            P.add("gpsimd", lambda e: e.dma_start(
                out=self.cent_b[:], in_=capf.rearrange("(kg p) m -> p kg m", p=128)),
                writes=(self.cent_r[0],), dma=True)
            self.ebias_bc, self.ebias_r = self.tile("ebias_bc", [128, 16], F32)
            ebap = self.dram_in("ebias", [1, 16])
            self.ebrow, self.ebrow_r = self.tile("ebrow", [1, 16], F32)
            P.add("sync", lambda e: e.dma_start(out=self.ebrow[:], in_=ebap[:]),
                  writes=(self.ebrow_r[0],), dma=True)
            P.add("tensor", lambda e: e.matmul(out=self.pb[0][0:128, 0:16],
                                               lhsT=self.ones_row_f32[:],
                                               rhs=self.ebrow[:], start=True, stop=True),
                  reads=(self.orf_r[0], self.ebrow_r[0]), writes=(self.pb_r[0],))
            P.add("vector", lambda e: e.tensor_copy(out=self.ebias_bc[:], in_=self.pb[0][0:128, 0:16]),
                  reads=(self.pb_r[0],), writes=(self.ebias_r[0],))
            self.br_b, self.br_r = self.tile("br_b", [16, 1024], BF16)
            brap = self.dram_in("brr", [16, 1024])
            P.add("gpsimd", lambda e: e.dma_start(out=self.br_b[:], in_=brap[:]),
                  writes=(self.br_r[0],), dma=True)
            self.bsh, self.bsh_r = self.tile("bsh", [128, 8], F32)
            self.bstmp, self.bstmp_r = self.tile("bstmp", [128, 2, 8], F32)
            bsap = self.dram_in("bs_pp", [2, 128, 8])
            P.add("sync", lambda e: e.dma_start(out=self.bstmp[:], in_=bsap[:].rearrange("e p g -> p e g")),
                  writes=(self.bstmp_r[0],), dma=True)
            def bs_build(e):
                e.tensor_tensor(out=self.bsh[:], in0=self.bstmp[:, 0, :], in1=self.bstmp[:, 1, :], op=ALU.add)
                e.drain()
                return e.tensor_scalar(out=self.bsh[:], in0=self.bsh[:], scalar1=0.5, scalar2=None, op0=ALU.mult)
            P.add("vector", bs_build, reads=(self.bstmp_r[0],), writes=(self.bsh_r[0],))
        if "l1" in need:
            self.gp, self.gp_r = self.tile("gp_sb", [128, 8], F32)
            gap = self.dram_in("gamma_pp", [128, 8])
            self.gstg, self.gstg_r = self.tile("gstg", [128, 8], F32)
            P.add("sync", lambda e: e.dma_start(out=self.gstg[:], in_=gap[:]),
                  writes=(self.gstg_r[0],), dma=True)
            P.add("vector", lambda e: e.tensor_scalar(out=self.gp[:], in0=self.gstg[:],
                                                      scalar1=1.0, scalar2=32.0,
                                                      op0=ALU.add, op1=ALU.mult),
                  reads=(self.gstg_r[0],), writes=(self.gp_r[0],))
        if "ffn1" in need:
            self.bias_pp, self.bias_r = self.tile("bias_sb", [128, 8], F32)
            bap = self.dram_in("bias_pp", [128, 8])
            P.add("sync", lambda e: e.dma_start(out=self.bias_pp[:], in_=bap[:]),
                  writes=(self.bias_r[0],), dma=True)
            self.alpha_t, self.alpha_r = self.tile("alpha_t", [1, 1], F32)
            aap = self.dram_in("alpha", [1, 1])
            P.add("sync", lambda e: e.dma_start(out=self.alpha_t[:], in_=aap[:]),
                  writes=(self.alpha_r[0],), dma=True)
            self.alph_bc, self.alphbc_r = self.tile("alph_bc", [128, 1], F32)
            P.add("tensor", lambda e: e.matmul(out=self.pb[1][0:128, 0:1],
                                               lhsT=self.ones_row_f32[:], rhs=self.alpha_t[:],
                                               start=True, stop=True),
                  reads=(self.orf_r[0], self.alpha_r[0]), writes=(self.pb_r[1],))
            P.add("vector", lambda e: e.tensor_copy(out=self.alph_bc[:], in_=self.pb[1][0:128, 0:1]),
                  reads=(self.pb_r[1],), writes=(self.alphbc_r[0],))

    def load_x_fp32(self):
        ap = self.dram_in("xT", [1024, T])
        P = self.P
        for g in range(G):
            for ch in range(0, NCH, 4):
                P.add("gpsimd", lambda e, g=g, ch=ch: e.dma_start(
                    out=self.xs[:, g, ch * TC:(ch + 4) * TC],
                    in_=ap[g * 128:(g + 1) * 128, ch * TC:(ch + 4) * TC]),
                    writes=tuple(self.xs_r[g][ch:ch + 4]), dma=True)

    def load_bf(self, name, dest, dres):
        ap = self.dram_in(name, [1024, T], BF16)
        P = self.P
        for g in range(G):
            for ch in range(0, NCH, 4):
                P.add("sync" if g % 2 == 0 else "gpsimd",
                      lambda e, g=g, ch=ch: e.dma_start(
                          out=dest[:, g, ch * TC:(ch + 4) * TC],
                          in_=ap[g * 128:(g + 1) * 128, ch * TC:(ch + 4) * TC]),
                      writes=tuple(dres[g][ch:ch + 4]), dma=True)

    def store_bf(self, name, src, sres):
        ap = self.dram_out(name, [1024, T], BF16)
        P = self.P
        for g in range(G):
            for ch in range(0, NCH, 4):
                P.add("sync" if g % 2 == 0 else "gpsimd",
                      lambda e, g=g, ch=ch: e.dma_start(
                          out=ap[g * 128:(g + 1) * 128, ch * TC:(ch + 4) * TC],
                          in_=src[:, g, ch * TC:(ch + 4) * TC]),
                      reads=tuple(sres[g][ch:ch + 4]), dma=True)

    def store_out_fp32(self):
        ap = self.dram_out("out", [1024, T])
        P = self.P
        for ch in range(0, NCH, 4):
            for g in range(G):
                P.add("gpsimd", lambda e, g=g, ch=ch: e.dma_start(
                    out=ap[g * 128:(g + 1) * 128, ch * TC:(ch + 4) * TC],
                    in_=self.xs[:, g, ch * TC:(ch + 4) * TC]),
                    reads=tuple(self.xs_r[g][ch:ch + 4]), dma=True)

    def phase_ln(self):
        P = self.P
        sq, sq_r = self.tile(f"lnsq{self.uid}", [128, 2, 512], BF16, 2)
        self.uid += 1
        for ch in range(NCH):
            par = ch % 2
            bA, bB, bR, bM = 4 * par, 4 * par + 1, 4 * par + 2, 4 * par + 3
            c0 = ch * TC
            cs = slice(c0, c0 + TC)
            for g in range(G):
                P.add("scalar", lambda e, g=g, cs=cs, par=par: e.square(
                    out=sq[:, par, :], in_=self.xs[:, g, cs]),
                    reads=(self.xs_r[g][ch],), writes=(sq_r[par],))
                P.add("tensor", lambda e, g=g, cs=cs, bA=bA: e.matmul(
                    out=self.pb[bA][0:1, :], lhsT=self.ones_col_bf[:, 0:1],
                    rhs=self.xs[:, g, cs], start=(g == 0), stop=(g == G - 1)),
                    reads=(self.ocb_r[0], self.xs_r[g][ch]), writes=(self.pb_r[bA],))
                P.add("tensor", lambda e, g=g, par=par, bB=bB: e.matmul(
                    out=self.pb[bB][0:1, :], lhsT=self.ones_col_bf[:, 0:1],
                    rhs=sq[:, par, :], start=(g == 0), stop=(g == G - 1)),
                    reads=(self.ocb_r[0], sq_r[par]), writes=(self.pb_r[bB],))
            sm = 0
            m_s, v_s, a_s, b_s = sm, sm + 1, sm + 2, sm + 3
            P.add("vector", lambda e, bA=bA, m_s=m_s: e.tensor_scalar(
                out=self.srow[:, m_s, :], in0=self.pb[bA][0:1, :], scalar1=1.0 / 1024, scalar2=None, op0=ALU.mult),
                reads=(self.pb_r[bA],), writes=(self.srow_r[m_s],))
            P.add("vector", lambda e, bB=bB, v_s=v_s: e.tensor_scalar(
                out=self.srow[:, v_s, :], in0=self.pb[bB][0:1, :], scalar1=1.0 / 1024, scalar2=None, op0=ALU.mult),
                reads=(self.pb_r[bB],), writes=(self.srow_r[v_s],))
            P.add("vector", lambda e, m_s=m_s, b_s=b_s: e.tensor_tensor(
                out=self.srow[:, b_s, :], in0=self.srow[:, m_s, :], in1=self.srow[:, m_s, :], op=ALU.mult),
                reads=(self.srow_r[m_s],), writes=(self.srow_r[b_s],))
            P.add("vector", lambda e, v_s=v_s, b_s=b_s: e.tensor_tensor(
                out=self.srow[:, v_s, :], in0=self.srow[:, v_s, :], in1=self.srow[:, b_s, :], op=ALU.subtract),
                reads=(self.srow_r[v_s], self.srow_r[b_s]), writes=(self.srow_r[v_s],))
            P.add("scalar", lambda e, v_s=v_s, a_s=a_s: e.activation(
                out=self.srow[:, a_s, :], in_=self.srow[:, v_s, :], func=AF.Sqrt, bias=self.cbias[0:1, 0:1]),
                reads=(self.srow_r[v_s], self.cbias_r[0]), writes=(self.srow_r[a_s],))
            P.add("vector", lambda e, a_s=a_s, b_s=b_s: e.reciprocal(
                out=self.srow[:, b_s, :], in_=self.srow[:, a_s, :]),
                reads=(self.srow_r[a_s],), writes=(self.srow_r[b_s],))
            P.add("vector", lambda e, m_s=m_s, b_s=b_s, a_s=a_s: e.tensor_tensor(
                out=self.srow[:, a_s, :], in0=self.srow[:, m_s, :], in1=self.srow[:, b_s, :], op=ALU.mult),
                reads=(self.srow_r[m_s], self.srow_r[b_s]), writes=(self.srow_r[a_s],))
            r_s, mr_s = b_s, a_s
            P.add("tensor", lambda e, r_s=r_s, bR=bR: e.matmul(
                out=self.pb[bR][:], lhsT=self.ones_row_f32[:], rhs=self.srow[:, r_s, :],
                start=True, stop=True),
                reads=(self.orf_r[0], self.srow_r[r_s]), writes=(self.pb_r[bR],))
            P.add("tensor", lambda e, mr_s=mr_s, bM=bM: e.matmul(
                out=self.pb[bM][:], lhsT=self.ones_row_f32[:], rhs=self.srow[:, mr_s, :],
                start=True, stop=True),
                reads=(self.orf_r[0], self.srow_r[mr_s]), writes=(self.pb_r[bM],))
            for g in range(G):
                fs = par
                P.add("vector", lambda e, g=g, cs=cs, bR=bR, fs=fs: e.tensor_tensor(
                    out=self.fstg[:, fs, :], in0=self.xs[:, g, cs], in1=self.pb[bR][:], op=ALU.mult),
                    reads=(self.xs_r[g][ch], self.pb_r[bR]), writes=(self.fstg_r[fs],))
                P.add("vector", lambda e, g=g, cs=cs, bM=bM, fs=fs: e.tensor_tensor(
                    out=self.xn[:, g, cs], in0=self.fstg[:, fs, :], in1=self.pb[bM][:], op=ALU.subtract),
                    reads=(self.fstg_r[fs], self.pb_r[bM]), writes=(self.xn_r[g][ch],))

    def phase_l0attn(self):
        nc, P = self.nc, self.P
        wq_b, wq_r = self.tile("wq_b", [128, G, 1024], BF16)
        self.load_w(self.dram_in("wqT", [1024, 1024]), wq_b, wq_r[0], G, 1024)
        w0ap = self.dram_in("w0T", [4, 256, 512])
        w2ap = self.dram_in("w2T", [4, 256, 512])
        w1ap = self.dram_in("w1T", [4, 512, 256])
        w0_b, w0_r = self.tile("w0_b", [128, 4, 2, 512], BF16)
        w2_b, w2_r = self.tile("w2_b", [128, 4, 2, 512], BF16)
        w1_b, w1_r = self.tile("w1_b", [128, 4, 4, 256], BF16)
        for h in range(4):
            self.load_w(w0ap[h], w0_b[:, h, :, :], w0_r[0], 2, 512)
            self.load_w(w2ap[h], w2_b[:, h, :, :], w2_r[0], 2, 512)
            self.load_w(w1ap[h], w1_b[:, h, :, :], w1_r[0], 4, 256)
        qw, qw_r = self.tile("qw", [128, G, 512], BF16, 1)
        sqq, sqq_r = self.tile("sqq", [128, 2, 512], BF16, 2)
        self.quad, self.quad_r = self.tile("quad", [4, 2, 512], F32, 2)
        self.quadb, self.quadb_r = self.tile("quadb", [4, 1, 512], BF16, 1)
        selw, selw_r = self.tile("selw", [128, G, 4], BF16, 1)
        P.add("vector", lambda e: e.memset(selw[:], 0.0), writes=(selw_r[0],))
        for g in range(G):
            P.add("scalar", lambda e, g=g: e.activation(
                out=selw[:, g, g // 2:g // 2 + 1], in_=self.qnw[:, g:g + 1], func=AF.Square),
                reads=(self.qnw_r[0],), writes=(selw_r[0],))
        sg, sg_r = self.tile("sg", [128, 2, 512], BF16, 2)
        zst, zst_r = self.tile("zst", [128, 2, 512], BF16, 2)

        for ch in range(NCH):
            c0 = ch * TC
            cs = slice(c0, c0 + TC)
            pend_s = None

            def mostats(mo, sl):
                P.add("tensor", lambda e, mo=mo, sl=sl: e.matmul(
                    out=self.pb[2][0:4, :], lhsT=self.ones_col_bf[:],
                    rhs=sqq[:, sl, :], start=(mo == 0), stop=(mo == G - 1)),
                    reads=(self.ocb_r[0], sqq_r[sl]), writes=(self.pb_r[2],))
                P.add("tensor", lambda e, mo=mo, sl=sl: e.matmul(
                    out=self.pb[3][0:4, :], lhsT=selw[:, mo, :],
                    rhs=sqq[:, sl, :], start=(mo == 0), stop=(mo == G - 1)),
                    reads=(selw_r[0], sqq_r[sl]), writes=(self.pb_r[3],))

            for mo in range(G):
                bq = mo % 2
                P.add("tensor", lambda e, mo=mo, cs=cs, bq=bq: [
                    e.matmul(out=self.pb[bq][:], lhsT=wq_b[:, kg, mo * 128:(mo + 1) * 128],
                             rhs=self.xn[:, kg, cs], start=(kg == 0), stop=(kg == G - 1))
                    for kg in range(G)][-1],
                    reads=(wq_r[0],) + tuple(self.xn_r[kg][ch] for kg in range(G)), writes=(self.pb_r[bq],))
                sl = mo % 2
                P.add("scalar", lambda e, mo=mo, bq=bq: e.activation(
                    out=qw[:, mo, :], in_=self.pb[bq][:], func=AF.Copy, bias=0.0, scale=self.qnw[:, mo:mo + 1]),
                    reads=(self.pb_r[bq], self.qnw_r[0]), writes=(qw_r[0],))
                P.add("scalar", lambda e, mo=mo, bq=bq, sl=sl: e.square(
                    out=sqq[:, sl, :], in_=self.pb[bq][:]),
                    reads=(self.pb_r[bq],), writes=(sqq_r[sl],))
                if pend_s is not None:
                    mostats(*pend_s)
                pend_s = (mo, sl)
            mostats(*pend_s)
            P.add("scalar", lambda e: e.activation(
                out=self.quad[:, 0, :], in_=self.pb[2][0:4, :], func=AF.Sqrt, bias=self.cbias[0:4, 1:2], scale=1.0 / 1024),
                reads=(self.pb_r[2], self.cbias_r[0]), writes=(self.quad_r[0],))
            P.add("vector", lambda e: e.reciprocal(out=self.quad[:, 1, :], in_=self.quad[:, 0, :]),
                  reads=(self.quad_r[0],), writes=(self.quad_r[1],))
            P.add("scalar", lambda e: e.activation(
                out=self.quad[:, 0, :], in_=self.pb[3][0:4, :], func=AF.Sqrt),
                reads=(self.pb_r[3],), writes=(self.quad_r[0],))
            def fchain(e):
                e.tensor_tensor(out=self.quad[:, 0, :], in0=self.quad[:, 0, :], in1=self.quad[:, 1, :], op=ALU.mult)
                e.drain()
                e.tensor_scalar(out=self.quad[:, 0, :], in0=self.quad[:, 0, :], scalar1=1e-5, scalar2=None, op0=ALU.add)
                e.drain()
                e.reciprocal(out=self.quad[:, 0, :], in_=self.quad[:, 0, :])
                e.drain()
                return e.tensor_tensor(out=self.quadb[:, 0, :], in0=self.quad[:, 1, :], in1=self.quad[:, 0, :], op=ALU.mult)
            P.add("vector", fchain, reads=(self.quad_r[0], self.quad_r[1]),
                  writes=(self.quad_r[0], self.quadb_r[0]))
            for g in range(G):
                P.add("tensor", lambda e, g=g: e.matmul(
                    out=self.pb[4][:], lhsT=self.self4[:, 128 * (g // 2):128 * (g // 2) + 128],
                    rhs=self.quadb[:, 0, :], start=True, stop=True),
                    reads=(self.self4_r[0], self.quadb_r[0]), writes=(self.pb_r[4],))
                P.add("vector", lambda e, g=g: e.tensor_tensor(
                    out=qw[:, g, :], in0=qw[:, g, :], in1=self.pb[4][:], op=ALU.mult),
                    reads=(qw_r[0], self.pb_r[4]), writes=(qw_r[0],))
            for h in range(4):
                pend_o = None

                def oacc(h2, mb, sl2):
                    for ob in range(2):
                        bo = 7 if ob else 4
                        P.add("tensor", lambda e, h2=h2, mb=mb, ob=ob, bo=bo, sl2=sl2: e.matmul(
                            out=self.pb[bo][:], lhsT=w1_b[:, h2, mb, ob * 128:(ob + 1) * 128],
                            rhs=zst[:, sl2, :], start=(mb == 0), stop=(mb == 3)),
                            reads=(w1_r[0], zst_r[sl2]), writes=(self.pb_r[bo],))

                for mb in range(4):
                    sl2 = mb % 2
                    bg, bh2 = (5, 6) if mb % 2 == 0 else (0, 1)
                    P.add("tensor", lambda e, h=h, mb=mb, bg=bg: [
                        e.matmul(out=self.pb[bg][:], lhsT=w0_b[:, h, kg, mb * 128:(mb + 1) * 128],
                                 rhs=qw[:, 2 * h + kg, :], start=(kg == 0), stop=(kg == 1))
                        for kg in range(2)][-1],
                        reads=(w0_r[0], qw_r[0]), writes=(self.pb_r[bg],))
                    P.add("tensor", lambda e, h=h, mb=mb, bh2=bh2: [
                        e.matmul(out=self.pb[bh2][:], lhsT=w2_b[:, h, kg, mb * 128:(mb + 1) * 128],
                                 rhs=qw[:, 2 * h + kg, :], start=(kg == 0), stop=(kg == 1))
                        for kg in range(2)][-1],
                        reads=(w2_r[0], qw_r[0]), writes=(self.pb_r[bh2],))
                    P.add("scalar", lambda e, sl2=sl2, bg=bg: e.activation(
                        out=sg[:, sl2, :], in_=self.pb[bg][:], func=AF.Silu),
                        reads=(self.pb_r[bg],), writes=(sg_r[sl2],))
                    P.add("vector", lambda e, sl2=sl2, bh2=bh2: e.tensor_tensor(
                        out=zst[:, sl2, :], in0=sg[:, sl2, :], in1=self.pb[bh2][:], op=ALU.mult),
                        reads=(sg_r[sl2], self.pb_r[bh2]), writes=(zst_r[sl2],))
                    if pend_o is not None:
                        oacc(*pend_o)
                    pend_o = (h, mb, sl2)
                oacc(*pend_o)
                for ob in range(2):
                    bo = 7 if ob else 4
                    P.add("vector", lambda e, h=h, ob=ob, cs=cs, bo=bo: e.tensor_tensor(
                        out=self.xs[:, 2 * h + ob, cs], in0=self.xs[:, 2 * h + ob, cs],
                        in1=self.pb[bo][:], op=ALU.add),
                        reads=(self.xs_r[2 * h + ob][ch], self.pb_r[bo]), writes=(self.xs_r[2 * h + ob][ch],))

    def phase_moe(self):
        nc, P = self.nc, self.P
        wrap = self.dram_in("WrT", [16, 1024, 1024])
        wsap = self.dram_in("WsT", [2, 1024, 1024])
        wr_b, wr_rs = self.tile("wr_b", [128, 2, G, 1024], BF16, 2)
        self.stg, self.stg_r = self.tile("stg", [128, 2, 256], F32, 2)
        gT, gT_r = self.tile("gT", [16, T], BF16, NCH)
        self._gT, self._gT_r = gT, gT_r
        su, su_r = self.tile("su", [128, 2, G, 512], BF16, 2)
        gbs, gbs_r = self.tile("gbs", [128, 2, 512], BF16, 2)
        grow, grow_r = self.tile("grow", [1, 2, 512], BF16, 2)
        for ch in range(NCH):
            for tt in range(TC // 128):
                tile0 = ch * TC + tt * 128
                tsl = slice(tile0, tile0 + 128)
                bS = tt % 2
                bT = 2 + tt % 2
                s_i = tt % 4
                P.add("tensor", lambda e, tsl=tsl, bS=bS: [
                    e.matmul(out=self.pb[bS][0:128, 0:16], lhsT=self.xn[:, kg, tsl],
                             rhs=self.cent_b[:, kg, :], start=(kg == 0), stop=(kg == G - 1))
                    for kg in range(G)][-1],
                    reads=(self.cent_r[0],) + tuple(self.xn_r[kg][ch] for kg in range(G)), writes=(self.pb_r[bS],))
                def gate_math(e, bS=bS, s_i=s_i):
                    sco = self.sq16[:, s_i, :]
                    c0 = self.col1[:, 2 * s_i:2 * s_i + 1]
                    e.tensor_tensor(out=sco, in0=self.pb[bS][0:128, 0:16], in1=self.ebias_bc[:], op=ALU.add)
                    e.drain()
                    return e.tensor_reduce(out=c0, in_=sco, axis=AX.X, op=ALU.max, negate=True)
                P.add("vector", gate_math, reads=(self.pb_r[bS], self.ebias_r[0]),
                      writes=(self.sq16_r[s_i], self.col1_r[2 * s_i], self.col1_r[2 * s_i + 1]))
                P.add("scalar", lambda e, s_i=s_i: e.activation(
                    out=self.sq16[:, s_i, :], in_=self.sq16[:, s_i, :], func=AF.Exp,
                    bias=self.col1[:, 2 * s_i:2 * s_i + 1], scale=1.0),
                    reads=(self.sq16_r[s_i], self.col1_r[2 * s_i]), writes=(self.sq16_r[s_i],))
                def gate_math2(e, s_i=s_i):
                    sco = self.sq16[:, s_i, :]
                    c0 = self.col1[:, 2 * s_i:2 * s_i + 1]
                    c1 = self.col1[:, 2 * s_i + 1:2 * s_i + 2]
                    e.tensor_reduce(out=c0, in_=sco, axis=AX.X, op=ALU.add)
                    e.drain()
                    e.reciprocal(out=c0, in_=c0)
                    e.drain()
                    e.tensor_scalar(out=sco, in0=sco, scalar1=c0, scalar2=None, op0=ALU.mult)
                    e.drain()
                    e.tensor_reduce(out=c0, in_=sco, axis=AX.X, op=ALU.max)
                    e.drain()
                    e.tensor_scalar(out=self.fstg[0:128, 0, 0:16], in0=sco, scalar1=c0,
                                    scalar2=-1e30, op0=ALU.is_ge, op1=ALU.mult)
                    e.drain()
                    e.tensor_tensor(out=self.fstg[0:128, 0, 0:16], in0=self.fstg[0:128, 0, 0:16],
                                    in1=sco, op=ALU.add)
                    e.drain()
                    e.tensor_reduce(out=c1, in_=self.fstg[0:128, 0, 0:16], axis=AX.X, op=ALU.max)
                    e.drain()
                    e.tensor_scalar(out=self.fstg[0:128, 0, 0:16], in0=sco, scalar1=c1,
                                    scalar2=None, op0=ALU.is_ge)
                    e.drain()
                    return e.tensor_tensor(out=sco, in0=sco, in1=self.fstg[0:128, 0, 0:16], op=ALU.mult)
                P.add("vector", gate_math2, reads=(self.sq16_r[s_i],),
                      writes=(self.sq16_r[s_i], self.col1_r[2 * s_i], self.col1_r[2 * s_i + 1],
                              self.fstg_r[0]))
                P.add("tensor", lambda e, s_i=s_i, bT=bT: e.matmul(
                    out=self.pb[bT][0:16, 0:128], lhsT=self.sq16[:, s_i, :], rhs=self.ident[:],
                    start=True, stop=True, is_transpose=True),
                    reads=(self.sq16_r[s_i], self.ident_r[0]), writes=(self.pb_r[bT],))
                P.add("scalar", lambda e, bT=bT, tsl=tsl: e.copy(
                    out=gT[:, tsl], in_=self.pb[bT][0:16, 0:128]),
                    reads=(self.pb_r[bT],), writes=(gT_r[ch],))
        for p in range(9):
            shared = (p == 8)
            for i, eidx in enumerate((2 * p, 2 * p + 1) if not shared else (0, 1)):
                src = wrap[eidx] if not shared else wsap[eidx]
                self.load_w(src, wr_b[:, i, :, :], wr_rs[i], G, 1024,
                            scale=None if not shared else 0.5)
            for ch in range(NCH):
                c0 = ch * TC
                cs = slice(c0, c0 + TC)
                if not shared:
                    for i in range(2):
                        eidx = 2 * p + i
                        P.add("scalar", lambda e, eidx=eidx, cs=cs, i=i: e.dma_start(
                            out=grow[:, i, :], in_=gT[eidx:eidx + 1, cs]),
                            reads=(gT_r[ch],), writes=(grow_r[i],), dma=True)
                        P.add("tensor", lambda e, i=i: e.matmul(
                            out=self.pb[6][:], lhsT=self.ones_row_bf[:], rhs=grow[:, i, :],
                            start=True, stop=True),
                            reads=(grow_r[i], self.orb_r[0]), writes=(self.pb_r[6],))
                        P.add("vector", lambda e, i=i: e.tensor_copy(out=gbs[:, i, :], in_=self.pb[6][:]),
                              reads=(self.pb_r[6],), writes=(gbs_r[i],))
                for i in range(2):
                    eidx = 2 * p + i
                    if not shared:
                        for kg in range(G):
                            P.add("vector", lambda e, kg=kg, cs=cs, i=i: e.tensor_tensor(
                                out=su[:, i, kg, :], in0=self.xn[:, kg, cs], in1=gbs[:, i, :], op=ALU.mult),
                                reads=(self.xn_r[kg][ch], gbs_r[i]), writes=(su_r[i],))
                        rd = lambda kg, cs, i=i: su[:, i, kg, :]
                    else:
                        rd = lambda kg, cs: self.xn[:, kg, cs]
                        rhs_res = None
                    for ob in range(8):
                        first = (i == 0)
                        last = (i == 1)
                        rres = (su_r[i],) if not shared else tuple(self.xn_r[kg][ch] for kg in range(G))
                        P.add("tensor", lambda e, i=i, ob=ob, cs=cs, rd=rd, first=first, last=last: [
                            e.matmul(out=self.pb[ob][:], lhsT=wr_b[:, i, kg, ob * 128:(ob + 1) * 128],
                                     rhs=rd(kg, cs), start=(first and kg == 0), stop=(last and kg == G - 1))
                            for kg in range(G)][-1],
                            reads=(wr_rs[i],) + rres, writes=(self.pb_r[ob],))
                if shared:
                    for ob in range(8):
                        P.add("tensor", lambda e, ob=ob, cs=cs: e.matmul(
                            out=self.pb[ob][:], lhsT=self.br_b[:, ob * 128:(ob + 1) * 128],
                            rhs=gT[:, cs], start=False, stop=False, skip_group_check=True),
                            reads=(self.br_r[0], gT_r[ch]), writes=(self.pb_r[ob],))
                for ob in range(8):
                    if not shared:
                        P.add("vector", lambda e, ob=ob, cs=cs: e.tensor_tensor(
                            out=self.xs[:, ob, cs], in0=self.xs[:, ob, cs], in1=self.pb[ob][:], op=ALU.add),
                            reads=(self.xs_r[ob][ch], self.pb_r[ob]), writes=(self.xs_r[ob][ch],))
                    else:
                        fs = ob % 2
                        P.add("vector", lambda e, ob=ob, cs=cs, fs=fs: e.scalar_tensor_tensor(
                            out=self.fstg[:, fs, :], in0=self.pb[ob][:], scalar=self.bsh[:, ob:ob + 1],
                            in1=self.xn[:, ob, cs], op0=ALU.add, op1=ALU.add),
                            reads=(self.pb_r[ob], self.bsh_r[0], self.xn_r[ob][ch]), writes=(self.fstg_r[fs],))
                        P.add("vector", lambda e, ob=ob, cs=cs, fs=fs: e.tensor_tensor(
                            out=self.xs[:, ob, cs], in0=self.xs[:, ob, cs], in1=self.fstg[:, fs, :], op=ALU.add),
                            reads=(self.xs_r[ob][ch], self.fstg_r[fs]), writes=(self.xs_r[ob][ch],))

    def phase_ffn1(self):
        nc, P = self.nc, self.P
        wfap = self.dram_in("wfT", [1024, 1024])
        wfstg, wfstg_r = self.tile("wfstg", [128, 2, 1024], F32, 2)
        tern, tern_r = self.tile("tern", [128, G, 1024], BF16, 1)
        wfabs, wfabs_r = self.tile("wfabs", [128, 9], F32)
        dcol, dcol_r = self.tile("dcol", [128, 2], F32)
        for kg in range(G):
            sl = kg % 2
            P.add("sync", lambda e, kg=kg, sl=sl: e.dma_start(
                out=wfstg[:, sl, :], in_=wfap[kg * 128:(kg + 1) * 128, :]),
                writes=(wfstg_r[sl],), dma=True)
            P.add("vector", lambda e, kg=kg, sl=sl: e.tensor_reduce(
                out=wfabs[:, kg:kg + 1], in_=wfstg[:, sl, :], axis=AX.X, op=ALU.add,
                apply_absolute_value=True),
                reads=(wfstg_r[sl],), writes=(wfabs_r[0],))
        P.add("vector", lambda e: e.tensor_reduce(
            out=wfabs[:, 8:9], in_=wfabs[:, 0:8], axis=AX.X, op=ALU.add),
            reads=(wfabs_r[0],), writes=(wfabs_r[0],))
        P.add("tensor", lambda e: e.matmul(
            out=self.pb[0][0:1, 0:1], lhsT=self.ones_col_f32[:], rhs=wfabs[:, 8:9],
            start=True, stop=True),
            reads=(wfabs_r[0], self.ocf_r[0]), writes=(self.pb_r[0],))
        P.add("vector", lambda e: e.tensor_scalar(
            out=wfabs[:, 8:9].rearrange("p c -> p c") if False else self.srow[:, 0, 0:1],
            in0=self.pb[0][0:1, 0:1], scalar1=0.7 / (1024.0 * 1024.0), scalar2=None, op0=ALU.mult),
            reads=(self.pb_r[0],), writes=(self.srow_r[0],))
        P.add("tensor", lambda e: e.matmul(
            out=self.pb[1][0:128, 0:1], lhsT=self.ones_row_f32[:], rhs=self.srow[:, 0, 0:1],
            start=True, stop=True),
            reads=(self.srow_r[0], self.orf_r[0]), writes=(self.pb_r[1],))
        def dbuild(e):
            e.tensor_copy(out=dcol[:, 0:1], in_=self.pb[1][0:128, 0:1])
            e.drain()
            return e.tensor_scalar(out=dcol[:, 1:2], in0=dcol[:, 0:1], scalar1=-1.0, scalar2=None, op0=ALU.mult)
        P.add("vector", dbuild, reads=(self.pb_r[1],), writes=(dcol_r[0],))
        for kg in range(G):
            sl = kg % 2
            P.add("sync", lambda e, kg=kg, sl=sl: e.dma_start(
                out=wfstg[:, sl, :], in_=wfap[kg * 128:(kg + 1) * 128, :]),
                writes=(wfstg_r[sl],), dma=True)
            def tern_build(e, kg=kg, sl=sl):
                e.tensor_scalar(out=tern[:, kg, :],
                                in0=wfstg[:, sl, :], scalar1=dcol[:, 0:1], scalar2=None, op0=ALU.is_gt)
                e.drain()
                e.tensor_scalar(out=wfstg[:, sl, :], in0=wfstg[:, sl, :], scalar1=dcol[:, 1:2],
                                scalar2=None, op0=ALU.is_lt)
                e.drain()
                return e.tensor_tensor(out=tern[:, kg, :], in0=tern[:, kg, :], in1=wfstg[:, sl, :],
                                       op=ALU.subtract)
            P.add("vector", tern_build, reads=(wfstg_r[sl], dcol_r[0]),
                  writes=(tern_r[0], wfstg_r[sl]))
        for ch in range(NCH):
            c0 = ch * TC
            cs = slice(c0, c0 + TC)
            for ob in range(8):
                bk = ob % 4 + 2
                P.add("tensor", lambda e, ob=ob, cs=cs, bk=bk: [
                    e.matmul(out=self.pb[bk][:], lhsT=tern[:, kg, ob * 128:(ob + 1) * 128],
                             rhs=self.xn[:, kg, cs], start=(kg == 0), stop=(kg == G - 1))
                    for kg in range(G)][-1],
                    reads=(tern_r[0],) + tuple(self.xn_r[kg][ch] for kg in range(G)), writes=(self.pb_r[bk],))
                fs = ob % 2
                P.add("vector", lambda e, ob=ob, bk=bk, fs=fs: e.tensor_scalar(
                    out=self.fstg[:, fs, :], in0=self.pb[bk][:], scalar1=self.alph_bc[:, 0:1],
                    scalar2=self.bias_pp[:, ob:ob + 1], op0=ALU.mult, op1=ALU.add),
                    reads=(self.pb_r[bk], self.alphbc_r[0], self.bias_r[0]), writes=(self.fstg_r[fs],))
                P.add("vector", lambda e, ob=ob, cs=cs, fs=fs: e.tensor_tensor(
                    out=self.xs[:, ob, cs], in0=self.xs[:, ob, cs], in1=self.fstg[:, fs, :], op=ALU.add),
                    reads=(self.xs_r[ob][ch], self.fstg_r[fs]), writes=(self.xs_r[ob][ch],))

    def phase_rms(self):
        P = self.P
        sq, sq_r = self.tile(f"rmssq{self.uid}", [128, 2, 512], BF16, 2)
        self.uid += 1
        for ch in range(NCH):
            par = ch % 2
            bA, bR = 4 * par, 4 * par + 2
            cs = slice(ch * TC, (ch + 1) * TC)
            for g in range(G):
                P.add("scalar", lambda e, g=g, cs=cs, par=par: e.square(
                    out=sq[:, par, :], in_=self.xn[:, g, cs]),
                    reads=(self.xn_r[g][ch],), writes=(sq_r[par],))
                P.add("tensor", lambda e, g=g, par=par, bA=bA: e.matmul(
                    out=self.pb[bA][0:1, :], lhsT=self.ones_col_bf[:, 0:1],
                    rhs=sq[:, par, :], start=(g == 0), stop=(g == G - 1)),
                    reads=(self.ocb_r[0], sq_r[par]), writes=(self.pb_r[bA],))
            sm = 0
            n_s, r_s = sm, sm + 1
            P.add("scalar", lambda e, bA=bA, n_s=n_s: e.activation(
                out=self.srow[:, n_s, :], in_=self.pb[bA][0:1, :], func=AF.Sqrt),
                reads=(self.pb_r[bA],), writes=(self.srow_r[n_s],))
            def rchain(e, n_s=n_s, r_s=r_s):
                e.tensor_scalar(out=self.srow[:, n_s, :], in0=self.srow[:, n_s, :],
                                scalar1=1e-12, scalar2=None, op0=ALU.max)
                e.drain()
                return e.reciprocal(out=self.srow[:, r_s, :], in_=self.srow[:, n_s, :])
            P.add("vector", rchain, reads=(self.srow_r[n_s],), writes=(self.srow_r[n_s], self.srow_r[r_s]))
            P.add("tensor", lambda e, r_s=r_s, bR=bR: e.matmul(
                out=self.pb[bR][:], lhsT=self.ones_row_f32[:], rhs=self.srow[:, r_s, :],
                start=True, stop=True),
                reads=(self.orf_r[0], self.srow_r[r_s]), writes=(self.pb_r[bR],))
            for g in range(G):
                P.add("vector", lambda e, g=g, cs=cs, bR=bR: e.scalar_tensor_tensor(
                    out=self.xn[:, g, cs], in0=self.xn[:, g, cs], scalar=self.gp[:, g:g + 1],
                    in1=self.pb[bR][:], op0=ALU.mult, op1=ALU.mult),
                    reads=(self.xn_r[g][ch], self.gp_r[0], self.pb_r[bR]), writes=(self.xn_r[g][ch],))

    def phase_attn1(self):
        nc, P = self.nc, self.P
        wqkvap = self.dram_in("wqkv", [1024, 3072])
        woap = self.dram_in("wout", [1024, 1024])
        wqk_b, wqk_r = self.tile("wqk_b", [128, G, 512], BF16, 1)
        wv_b, wv_r = self.tile("wv_b", [128, G, 256], BF16, 1)
        wo_b, wo_r = self.tile("wo_b", [128, 2, 1024], BF16, 1)
        kaug, kaug_r = self.tile("kaug", [65, 4, 2048], BF16, 4)
        qaug, qaug_r = self.tile("qaug", [65, 4, 512], BF16, 4)
        vaug, vaug_r = self.tile("vaug", [128, 16, 4, 65], BF16, 16)
        sexp, sexp_r = self.tile("sexp", [128, 2, 1024], BF16, 2)
        sqst, sqst_r = self.tile("sqst", [64, 2, 512], BF16, 2)
        maxk, maxk_r = self.tile("maxk", [1, 8], F32, 5)
        rbrow, rb_r = self.tile("rbrow", [1, 1, 512], F32, 1)
        rbs, rbs_r = self.tile("rbs", [64, 1, 512], BF16, 1)
        ohb, ohb_r = self.tile("ohb", [128, 2, 4, 512], BF16, 8)

        def ohres(h, ic):
            return ohb_r[(h // 2) * 4 + ic]

        P.add("vector", lambda e: e.memset(kaug[64:65, :, :], 1.0),
              writes=tuple(kaug_r))
        P.add("vector", lambda e: e.memset(vaug[:, :, :, 64:65], 1.0),
              writes=tuple(vaug_r))
        for b in range(2):
            for hg in range(4):
                h0 = 4 * hg
                qk = wqkvap[:, 64 * h0:64 * h0 + 256]
                kk = wqkvap[:, 1024 + 64 * h0:1024 + 64 * h0 + 256]
                vv = wqkvap[:, 2048 + 64 * h0:2048 + 64 * h0 + 256]
                for (colap, dst0) in ((qk, 0), (kk, 256)):
                    P.add("gpsimd", lambda e, colap=colap, dst0=dst0: e.dma_start(
                        out=wqk_b[:, :, dst0:dst0 + 256],
                        in_=colap.rearrange("(kg p) m -> p kg m", p=128)),
                        writes=(wqk_r[0],), dma=True)
                P.add("gpsimd", lambda e, vv=vv: e.dma_start(
                    out=wv_b[:], in_=vv.rearrange("(kg p) m -> p kg m", p=128)),
                    writes=(wv_r[0],), dma=True)
                self.load_w(woap[256 * hg:256 * (hg + 1), :], wo_b, wo_r[0], 2, 1024)
                t0 = b * 2048
                for jc in range(4):
                    cs = slice(t0 + jc * TC, t0 + (jc + 1) * TC)
                    jsl = slice(jc * TC, (jc + 1) * TC)
                    xnr = tuple(self.xn_r[kg][4 * b + jc] for kg in range(G))
                    pend_k = None

                    def kstats(h, sl, jc=jc):
                        P.add("tensor", lambda e, sl=sl: e.matmul(
                            out=self.pb[2][0:1, :], lhsT=self.ones_col_bf[0:64, 0:1],
                            rhs=sqst[:, sl, :], start=True, stop=True),
                            reads=(sqst_r[sl], self.ocb_r[0]), writes=(self.pb_r[2],))
                        P.add("scalar", lambda e: e.activation(
                            out=self.srow[:, 3, :], in_=self.pb[2][0:1, :], func=AF.Sqrt),
                            reads=(self.pb_r[2],), writes=(self.srow_r[3],))
                        P.add("vector", lambda e: e.tensor_reduce(
                            out=maxk[:, 4:5], in_=self.srow[:, 3, :], axis=AX.X, op=ALU.max),
                            reads=(self.srow_r[3],), writes=(maxk_r[4],))
                        if jc == 0:
                            P.add("vector", lambda e, h=h: e.tensor_copy(
                                out=maxk[:, h:h + 1], in_=maxk[:, 4:5]),
                                reads=(maxk_r[4],), writes=(maxk_r[h],))
                        else:
                            P.add("vector", lambda e, h=h: e.tensor_tensor(
                                out=maxk[:, h:h + 1], in0=maxk[:, h:h + 1], in1=maxk[:, 4:5], op=ALU.max),
                                reads=(maxk_r[4], maxk_r[h]), writes=(maxk_r[h],))

                    for h in range(4):
                        bK = h % 2
                        P.add("tensor", lambda e, h=h, cs=cs, bK=bK: [
                            e.matmul(out=self.pb[bK][0:64, :], lhsT=wqk_b[:, kg, 256 + 64 * h:256 + 64 * h + 64],
                                     rhs=self.xn[:, kg, cs], start=(kg == 0), stop=(kg == G - 1))
                            for kg in range(G)][-1],
                            reads=(wqk_r[0],) + xnr, writes=(self.pb_r[bK],))
                        P.add("scalar", lambda e, h=h, jsl=jsl, bK=bK: e.copy(
                            out=kaug[0:64, h, jsl], in_=self.pb[bK][0:64, :]),
                            reads=(self.pb_r[bK],), writes=(kaug_r[h],))
                        P.add("scalar", lambda e, bK=bK: e.square(
                            out=sqst[:, bK, :], in_=self.pb[bK][0:64, :]),
                            reads=(self.pb_r[bK],), writes=(sqst_r[bK],))
                        if pend_k is not None:
                            kstats(*pend_k)
                        pend_k = (h, h % 2)
                    kstats(*pend_k)
                    for tt2 in range(4):
                        jb = jc * 4 + tt2
                        tsl = slice(t0 + jb * 128, t0 + (jb + 1) * 128)
                        bV = 3 if tt2 % 2 == 0 else 4
                        P.add("tensor", lambda e, tsl=tsl, bV=bV: [
                            e.matmul(out=self.pb[bV][0:128, 0:256], lhsT=self.xn[:, kg, tsl],
                                     rhs=wv_b[:, kg, :], start=(kg == 0), stop=(kg == G - 1))
                            for kg in range(G)][-1],
                            reads=(wv_r[0],) + xnr, writes=(self.pb_r[bV],))
                        def vevac(e, jb=jb, bV=bV):
                            for h in range(4):
                                i = e.copy(out=vaug[:, jb, h, 0:64],
                                           in_=self.pb[bV][0:128, 64 * h:64 * h + 64])
                            return i
                        P.add("scalar", vevac, reads=(self.pb_r[bV],), writes=(vaug_r[jb],))
                for h in range(4):
                    if ATTN_LVL < 2:
                        break
                    pend_q = None

                    def qstats(ic, sl, h=h):
                        P.add("tensor", lambda e, sl=sl: e.matmul(
                            out=self.pb[2][0:1, :], lhsT=self.ones_col_bf[0:64, 0:1],
                            rhs=sqst[:, sl, :], start=True, stop=True),
                            reads=(sqst_r[sl], self.ocb_r[0]), writes=(self.pb_r[2],))
                        P.add("scalar", lambda e: e.activation(
                            out=self.srow[:, 3, :], in_=self.pb[2][0:1, :], func=AF.Sqrt),
                            reads=(self.pb_r[2],), writes=(self.srow_r[3],))
                        P.add("vector", lambda e, h=h, ic=ic: e.tensor_scalar(
                            out=self.srow[:, 2, :], in0=self.srow[:, 3, :],
                            scalar1=maxk[:, h:h + 1], scalar2=-1.0, op0=ALU.mult, op1=ALU.mult),
                            reads=(self.srow_r[3], maxk_r[h]), writes=(self.srow_r[2],))
                        P.add("scalar", lambda e, ic=ic: e.copy(
                            out=qaug[64:65, ic, :], in_=self.srow[:, 2, :]),
                            reads=(self.srow_r[2],), writes=(qaug_r[ic],))

                    for ic in range(4):
                        ics = slice(t0 + ic * TC, t0 + (ic + 1) * TC)
                        bQ = 3 if ic % 2 == 0 else 4
                        xnr = tuple(self.xn_r[kg][4 * b + ic] for kg in range(G))
                        P.add("tensor", lambda e, h=h, ics=ics, bQ=bQ: [
                            e.matmul(out=self.pb[bQ][0:64, :], lhsT=wqk_b[:, kg, 64 * h:64 * h + 64],
                                     rhs=self.xn[:, kg, ics], start=(kg == 0), stop=(kg == G - 1))
                            for kg in range(G)][-1],
                            reads=(wqk_r[0],) + xnr, writes=(self.pb_r[bQ],))
                        P.add("scalar", lambda e, ic=ic, bQ=bQ: e.activation(
                            out=qaug[0:64, ic, :], in_=self.pb[bQ][0:64, :], func=AF.Copy,
                            bias=0.0, scale=0.125),
                            reads=(self.pb_r[bQ],), writes=(qaug_r[ic],))
                        P.add("scalar", lambda e, ic=ic, bQ=bQ: e.activation(
                            out=sqst[:, ic % 2, :], in_=self.pb[bQ][0:64, :], func=AF.Square, scale=0.125),
                            reads=(self.pb_r[bQ],), writes=(sqst_r[ic % 2],))
                        if pend_q is not None:
                            qstats(*pend_q)
                        pend_q = (ic, ic % 2)
                    qstats(*pend_q)
                    def emit_norm(ic, bO, h=h):
                        P.add("scalar", lambda e, bO=bO, ic=ic: e.activation(
                            out=self.srow[:, 1, :], in_=self.pb[bO][64:65, :], func=AF.Ln,
                            scale=1099511627776.0),
                            reads=(self.pb_r[bO],), writes=(self.srow_r[1],))
                        P.add("scalar", lambda e, ic=ic: e.activation(
                            out=rbrow[:, 0, :], in_=self.srow[:, 1, :], func=AF.Exp,
                            bias=self.cbias[0:1, 2:3], scale=-1.0),
                            reads=(self.srow_r[1], self.cbias_r[0]), writes=(rb_r[0],))
                        P.add("tensor", lambda e, ic=ic: e.matmul(
                            out=self.pb[2][0:64, :], lhsT=self.ones_row_f32[0:1, 0:64],
                            rhs=rbrow[:, 0, :], start=True, stop=True),
                            reads=(rb_r[0], self.orf_r[0]), writes=(self.pb_r[2],))
                        P.add("scalar", lambda e, ic=ic: e.copy(
                            out=rbs[:, 0, :], in_=self.pb[2][0:64, :]),
                            reads=(self.pb_r[2],), writes=(rbs_r[0],))
                        P.add("vector", lambda e, h=h, ic=ic, bO=bO: e.tensor_tensor(
                            out=ohb[64 * (h % 2):64 * (h % 2) + 64, h // 2, ic, :],
                            in0=self.pb[bO][0:64, :], in1=rbs[:, 0, :], op=ALU.mult),
                            reads=(self.pb_r[bO], rbs_r[0]), writes=(ohres(h, ic),))

                    for icp in (0, 2):
                        if ATTN_LVL < 3:
                            break
                        def sim_pair(pr, s):
                            bb = 0 if s == 0 else 6
                            ic = icp + s
                            for half in range(2):
                                jb = 2 * pr + half
                                P.add("tensor", lambda e, h=h, jb=jb, bb=bb, half=half, ic=ic: e.matmul(
                                    out=self.pb[bb + half][:], lhsT=kaug[:, h, jb * 128:(jb + 1) * 128],
                                    rhs=qaug[:, ic, :], start=True, stop=True),
                                    reads=(kaug_r[h], qaug_r[ic]), writes=(self.pb_r[bb + half],))
                        sim_pair(0, 0)
                        sim_pair(0, 1)
                        for pr in range(8):
                            for s in range(2):
                                bb = 0 if s == 0 else 6
                                bO = 3 + s
                                P.add("scalar", lambda e, s=s, bb=bb: e.activation(
                                    out=sexp[:, s, :], in_=self.pbH[bb // 2][:, 0:1024], func=AF.Exp),
                                    reads=(self.pb_r[bb], self.pb_r[bb + 1]), writes=(sexp_r[s],))
                                if pr < 7:
                                    sim_pair(pr + 1, s)
                                for half in range(2):
                                    jb = 2 * pr + half
                                    P.add("tensor", lambda e, h=h, jb=jb, s=s, half=half, bO=bO: e.matmul(
                                        out=self.pb[bO][0:65, :], lhsT=vaug[:, jb, h, :],
                                        rhs=sexp[:, s, half * 512:half * 512 + 512],
                                        start=(jb == 0), stop=(jb == 15)),
                                        reads=(vaug_r[jb], sexp_r[s]), writes=(self.pb_r[bO],))
                        if ATTN_LVL < 4:
                            continue
                        emit_norm(icp, 3)
                        emit_norm(icp + 1, 4)
                for ic in range(4):
                    if ATTN_LVL < 5:
                        break
                    ics = slice(t0 + ic * TC, t0 + (ic + 1) * TC)
                    for ob in range(8):
                        bk = 2 if (ic * 8 + ob) % 2 == 0 else 5
                        P.add("tensor", lambda e, ob=ob, bk=bk, ic=ic: [
                            e.matmul(out=self.pb[bk][:], lhsT=wo_b[:, kg, ob * 128:(ob + 1) * 128],
                                     rhs=ohb[:, kg, ic, :], start=(kg == 0), stop=(kg == 1))
                            for kg in range(2)][-1],
                            reads=(wo_r[0], ohres(0, ic), ohres(2, ic)), writes=(self.pb_r[bk],))
                        if ATTN_LVL >= 6:
                            P.add("vector", lambda e, ob=ob, ics=ics, bk=bk, ic=ic: e.tensor_tensor(
                                out=self.xs[:, ob, ics], in0=self.xs[:, ob, ics], in1=self.pb[bk][:], op=ALU.add),
                                reads=(self.xs_r[ob][4 * b + ic], self.pb_r[bk]), writes=(self.xs_r[ob][4 * b + ic],))



def build_nc(stage):
    nc = bass.Bass("TRN2", target_bir_lowering=False)
    P = Prog()
    with contextlib.ExitStack() as st:
        kb = KB(nc, P, st)
        kb.alloc_global()
        if stage == "A":
            kb.load_params(("l0",))
            kb.load_x_fp32()
            kb.phase_ln()
            kb.phase_l0attn()
            kb.store_bf("x_out", kb.xs, kb.xs_r)
            kb.store_bf("xn_out", kb.xn, kb.xn_r)
        elif stage == "B":
            kb.load_params(("moe",))
            kb.load_bf("x_in", kb.xs, kb.xs_r)
            kb.load_bf("xn_in", kb.xn, kb.xn_r)
            kb.phase_moe()
            kb.store_bf("x_out", kb.xs, kb.xs_r)
            if getattr(build_nc, "debug_gates", False):
                gap = kb.dram_out("g_dbg", [16, T], BF16)
                gt = kb._gT
                for ch in range(NCH):
                    kb.P.add("gpsimd", lambda e, ch=ch: e.dma_start(
                        out=gap[:, ch * TC:(ch + 1) * TC], in_=gt[:, ch * TC:(ch + 1) * TC]),
                        reads=(kb._gT_r[ch],), dma=True)
        elif stage == "C":
            kb.load_params(("l1", "ffn1"))
            kb.load_bf("x_in", kb.xs, kb.xs_r)
            kb.phase_ln()
            kb.phase_ffn1()
            kb.phase_rms()
            kb.store_bf("x_out", kb.xs, kb.xs_r)
            kb.store_bf("xn_out", kb.xn, kb.xn_r)
        elif stage == "D":
            kb.load_params(("l1",))
            kb.load_bf("x_in", kb.xs, kb.xs_r)
            kb.load_bf("xn_in", kb.xn, kb.xn_r)
            kb.phase_attn1()
            kb.store_out_fp32()
        elif stage == "FUSED":
            kb.load_params(("l0", "moe", "l1", "ffn1"))
            kb.load_x_fp32()
            with kb.scope("left"):
                kb.phase_ln()
                kb.phase_l0attn()
            kb.P.fence_close("A")
            kb.P.fence_open("A")
            with kb.scope("right"):
                kb.phase_moe()
            kb.P.fence_close("B")
            kb.P.fence_open("B")
            with kb.scope("left"):
                kb.phase_ln()
                kb.phase_ffn1()
                kb.phase_rms()
            kb.P.fence_close("C")
            kb.P.fence_open("C")
            with kb.scope("right"):
                kb.phase_attn1()
            kb.store_out_fp32()
        P.emit(nc, st)
    return nc



def _prep_weights(inp):
    C = np.ascontiguousarray
    w = {}
    _dead = (inp["L0_attn_lr_proj_w"], inp["L0_attn_lr_proj_b"], inp["L0_attn_k_norm_w"])
    w["wqT"] = C(inp["L0_attn_qkv_w"][0:1024].T)
    w["qnw_pp"] = C(inp["L0_attn_q_norm_w"].reshape(8, 128).T)
    w["w0T"] = C(inp["L0_attn_w0"].transpose(0, 2, 1))
    w["w1T"] = C(inp["L0_attn_w1"].transpose(0, 2, 1))
    w["w2T"] = C(inp["L0_attn_w2"].transpose(0, 2, 1))
    w["centT"] = C(inp["L0_ffn_centroids"].T)
    w["ebias"] = C(inp["L0_ffn_expert_biases"].reshape(1, 16))
    w["WrT"] = C(inp["L0_ffn_Wr"].transpose(0, 2, 1))
    w["brr"] = C(inp["L0_ffn_br"])
    w["WsT"] = C(inp["L0_ffn_Ws"].transpose(0, 2, 1))
    w["bs_pp"] = C(inp["L0_ffn_bs"].reshape(2, 8, 128).transpose(0, 2, 1))
    w["gamma_pp"] = C(inp["L1_attn_gamma"].reshape(8, 128).T)
    w["wqkv"] = C(inp["L1_attn_w_qkv"])
    w["wout"] = C(inp["L1_attn_w_out"])
    w["wfT"] = C(inp["L1_ffn_weight"].T)
    w["alpha"] = C(inp["L1_ffn_alpha"].reshape(1, 1))
    w["bias_pp"] = C(inp["L1_ffn_bias"].reshape(8, 128).T)
    return w

_REQ = {
    "A": ["xT", "wqT", "qnw_pp", "w0T", "w1T", "w2T"],
    "B": ["x_in", "xn_in", "centT", "ebias", "WrT", "brr", "WsT", "bs_pp"],
    "C": ["x_in", "gamma_pp", "wfT", "alpha", "bias_pp"],
    "D": ["x_in", "xn_in", "gamma_pp", "wqkv", "wout"],
    "FUSED": ["xT", "wqT", "qnw_pp", "w0T", "w1T", "w2T", "centT", "ebias", "WrT", "brr",
              "WsT", "bs_pp", "gamma_pp", "wqkv", "wout", "wfT", "alpha", "bias_pp"],
}


STAGED = False

def kernel(**inp):
    w = _prep_weights(inp)
    x = inp["L0_attn_hidden_states"]
    xT = [np.ascontiguousarray(x[2 * c:2 * c + 2].reshape(T, 1024).T) for c in range(NCORES)]

    state = {c: {"xT": xT[c]} for c in range(NCORES)}
    stages = ("A", "B", "C", "D") if STAGED else ("FUSED",)
    for stage in stages:
        nc = build_nc(stage)
        in_maps = []
        for c in range(NCORES):
            m = {}
            for k in _REQ[stage]:
                m[k] = state[c][k] if k in ("xT", "x_in", "xn_in") else w[k]
            in_maps.append(m)
        res = run_bass_kernel_spmd(nc, in_maps, core_ids=list(range(NCORES)))
        for c in range(NCORES):
            r = res.results[c]
            if "x_out" in r:
                state[c]["x_in"] = r["x_out"]
            if "xn_out" in r:
                state[c]["xn_in"] = r["xn_out"]
            if "out" in r:
                state[c]["out"] = r["out"]

    out = np.empty((16, 2048, 1024), np.float32)
    for c in range(NCORES):
        out[2 * c:2 * c + 2] = np.asarray(state[c]["out"], np.float32).T.reshape(2, 2048, 1024)
    return out


